# version 06 (final)
# baseline (speedup 1.0000x reference)
import numpy as np
import ml_dtypes
import concourse.bass as bass
import concourse.mybir as mybir
from concourse.bass_utils import run_bass_kernel_spmd

F32 = mybir.dt.float32
BF16 = mybir.dt.bfloat16
AF = mybir.ActivationFunctionType
ALU = mybir.AluOpType
bf16 = ml_dtypes.bfloat16

D = 2048
DT = D // 128
NH = 16
B, S = 4, 4096
EPS = 1e-5
CH = 512

ENGINES = ("sync", "gpsimd", "tensor", "vector", "scalar")


def f_layout(x_td):
    T = x_td.shape[0]
    return np.ascontiguousarray(x_td.T.reshape(DT, 128, T))


def un_f_layout(x_f):
    return np.ascontiguousarray(x_f.reshape(D, -1).T)


def prep_lhsT(w_t):
    K, M = w_t.shape
    KO, MO = K // 128, M // 128
    return np.ascontiguousarray(w_t.reshape(KO, 128, MO, 128).transpose(2, 1, 0, 3))


def col_tiles(v):
    return np.ascontiguousarray(v.reshape(-1, 128).T)


class Sems:
    def __init__(self, nc, names):
        self.h = {}
        self.v = {}
        self._cms = []
        for n in names:
            cm = nc.semaphore(n)
            self.h[n] = cm.__enter__()
            self._cms.append(cm)
            self.v[n] = 0


SEM_NAMES = (["sd", "gd", "mm", "ev", "es", "x1", "x2", "x3"] +
             [f"p{i}" for i in range(24)])


class VSync:

    def __init__(self, S, sem="x1"):
        self.S = S
        self.sem = sem
        if not hasattr(S, "vsn"):
            S.vsn = {}
        S.vsn.setdefault(sem, 0)

    def g(self, e, inst):
        self.S.vsn[self.sem] += 1
        inst.then_inc(self.S.h[self.sem], 1)
        e.wait_ge(self.S.h[self.sem], self.S.vsn[self.sem])


class Ph:

    def __init__(self, nc, sems):
        self.nc = nc
        self.S = sems
        self.steps = {e: [] for e in ENGINES}
        self._allocs = []

    def sbuf(self, name, shape, dt):
        cm = self.nc.sbuf_tensor(name, shape, dt)
        h = cm.__enter__()
        self._allocs.append(cm)
        return h

    def psum(self, name, shape, dt=F32):
        cm = self.nc.psum_tensor(name, shape, dt)
        h = cm.__enter__()
        self._allocs.append(cm)
        return h

    def tok(self, sem, n=1):
        self.S.v[sem] += n
        return (sem, self.S.v[sem])

    def on(self, eng, fn, wait=(), inc=()):
        self.steps[eng].append((list(wait), fn, list(inc)))

    def step(self, eng, fn, wait=(), sem=None):
        if sem is None:
            self.on(eng, fn, wait=wait)
            return None
        t = self.tok(sem, 1)
        self.on(eng, fn, wait=wait, inc=[(sem, 1)])
        return t

    def dma(self, eng, out, in_, wait=(), slow=False, sem=None):
        if sem is None:
            sem = "sd" if eng == "sync" else "gd"
        t = self.tok(sem, 16)
        self.on(eng, lambda e, o=out, i=in_, s=slow: e.dma_start(
            o, i, allow_slow_non_contiguous=s), wait=wait, inc=[(sem, 16)])
        return t

    def emit(self, blk):
        for eng in ENGINES:
            steps = self.steps[eng]
            if not steps:
                continue

            def body(e, steps=steps):
                hi = {}
                for waits, fn, incs in steps:
                    for semn, val in waits:
                        if hi.get(semn, -1) >= val:
                            continue
                        e.wait_ge(self.S.h[semn], val)
                        hi[semn] = val
                    inst = fn(e)
                    for semn, dv in incs:
                        assert inst is not None, "inc on step with no inst"
                        inst = inst.then_inc(self.S.h[semn], dv)
            getattr(blk, eng)(body)


def run_phase(nc, sems, build, *a, **k):
    with nc.Block() as blk:
        p = Ph(nc, sems)
        build(p, *a, **k)
        p.emit(blk)
    for cm in reversed(p._allocs):
        cm.__exit__(None, None, None)
    return p


def finish_launch(nc, sems):
    def fin(p):
        allw = [(n, sems.v[n]) for n in sems.v
                if (n.startswith("p") or n in ("sd", "gd")) and sems.v[n] > 0]
        p.on("sync", lambda e: None, wait=allw)
        p.on("gpsimd", lambda e: None, wait=allw)
    run_phase(nc, sems, fin)


def const_inputs():
    n = np.arange(128)
    ang = 2.0 * np.pi * np.outer(n, n) / 128.0
    return {
        "dftC": np.cos(ang).astype(np.float32),
        "dftS": np.sin(ang).astype(np.float32),
        "ident32": np.eye(32, dtype=np.float32),
    }


def declare_consts(nc):
    return {
        "dftC": nc.dram_tensor("dftC", [128, 128], F32, kind="ExternalInput"),
        "dftS": nc.dram_tensor("dftS", [128, 128], F32, kind="ExternalInput"),
        "ident32": nc.dram_tensor("ident32", [32, 32], F32, kind="ExternalInput"),
    }


class Consts:

    def __init__(self, nc):
        A = nc.alloc_sbuf_tensor
        self.ones_col_f = A("c_ones_col_f", [128, 1], F32)
        self.ones_col_b = A("c_ones_col_b", [128, 1], BF16)
        self.ones_row_f = A("c_ones_row_f", [1, 128], F32)
        self.ones_row_b = A("c_ones_row_b", [1, 128], BF16)
        self.dftC = A("c_dftC", [128, 128], BF16)
        self.dftS = A("c_dftS", [128, 128], BF16)
        self.id32 = A("c_id32", [32, 32], F32)

    def load(self, p, cd):
        for t in (self.ones_col_f, self.ones_row_f, self.ones_col_b,
                  self.ones_row_b):
            p.step("vector", lambda e, t=t: e.memset(t[:], 1.0))
        p.dma("gpsimd", self.dftC[:], cd["dftC"].ap())
        p.dma("gpsimd", self.dftS[:], cd["dftS"].ap())
        p.dma("gpsimd", self.id32[:], cd["ident32"].ap())


def ph_ln(p, C, *, x, xn, mean_row, rstd_row, T, tag):
    NCk = T // CH
    _xs = p.sbuf(f"lnx{tag}0", [128, DT, CH], F32)
    _xo = p.sbuf(f"lno{tag}0", [128, DT, CH], BF16)
    xs = [_xs, _xs]
    xo = [_xo, _xo]
    sqf = p.sbuf(f"lnsq{tag}", [128, DT, CH], F32)
    r1 = p.sbuf(f"lnr1{tag}", [1, CH], F32)
    r2 = p.sbuf(f"lnr2{tag}", [1, CH], F32)
    p1 = p.psum(f"lnp1{tag}", [1, CH])
    p2 = p.psum(f"lnp2{tag}", [1, CH])
    pa = p.psum(f"lnpa{tag}", [128, CH])
    pb = p.psum(f"lnpb{tag}", [128, CH])

    xr = x.rearrange("a p t -> p a t")
    xnr = xn.rearrange("a p t -> p a t")
    freeX = [None, None]
    freeO = [None, None]
    tMean_prev = tRows_prev = tEv_prev = tSq_prev = None
    for c in range(NCk):
        sl = slice(c * CH, (c + 1) * CH)
        bi = c % 2
        w = [t for t in (freeX[0], freeX[1]) if t]
        tL = p.dma("sync", xs[bi][:], xr[:, :, sl], wait=w, sem="p8")
        wq = [tL] + ([tSq_prev] if tSq_prev else [])
        tQ = p.step("scalar", lambda e, bi=bi: e.activation(
            sqf[:].rearrange("p a t -> p (a t)"),
            xs[bi][:].rearrange("p a t -> p (a t)"), AF.Square),
            wait=wq, sem="es")
        def mmsum(e, bi=bi):
            last = None
            for d in range(DT):
                last = e.matmul(p1[:], C.ones_col_f[:], xs[bi][:, d, :],
                                start=(d == 0), stop=(d == DT - 1))
            return last
        wm = [tL] + ([tMean_prev] if tMean_prev else [])
        tM1 = p.step("tensor", mmsum, wait=wm, sem="mm")

        def mmsq(e):
            last = None
            for d in range(DT):
                last = e.matmul(p2[:], C.ones_col_f[:], sqf[:, d, :],
                                start=(d == 0), stop=(d == DT - 1))
            return last
        wm2 = [tQ] + ([tRows_prev] if tRows_prev else [])
        tM2 = p.step("tensor", mmsq, wait=wm2, sem="mm")
        tSq_prev = tM2

        tMean = p.step("scalar", lambda e, sl=sl: e.mul(
            mean_row[:, sl], p1[:], 1.0 / D), wait=[tM1], sem="es")
        tMean_prev = tMean
        vs = VSync(p.S)
        def rowa(e, sl=sl, vs=vs):
            vs.g(e, e.tensor_mul(r1[:], mean_row[:, sl], mean_row[:, sl]))
            vs.g(e, e.tensor_scalar(r1[:], r1[:], EPS, None, ALU.subtract))
            return e.scalar_tensor_tensor(
                r2[:], p2[:], 1.0 / D, r1[:], ALU.mult, ALU.subtract)
        tV = p.step("vector", rowa, wait=[tMean, tM2], sem="ev")
        tSd = p.step("scalar", lambda e: e.activation(
            r1[:], r2[:], AF.Sqrt), wait=[tV], sem="es")
        def rowb(e, sl=sl, vs=vs):
            vs.g(e, e.reciprocal(rstd_row[:, sl], r1[:]))
            return e.scalar_tensor_tensor(
                r2[:], mean_row[:, sl], -1.0, rstd_row[:, sl], ALU.mult,
                ALU.mult)
        tRows = p.step("vector", rowb, wait=[tSd], sem="ev")
        tRows_prev = tRows

        def mmb(e, sl=sl):
            e.matmul(pa[:], C.ones_row_f[:], rstd_row[:, sl], start=True,
                     stop=True)
            return e.matmul(pb[:], C.ones_row_f[:], r2[:], start=True,
                            stop=True)
        wb = [tRows] + ([tEv_prev] if tEv_prev else [])
        tB = p.step("tensor", mmb, wait=wb, sem="mm")

        wo = [t for t in (freeO[0], freeO[1], tB) if t]
        p.step("vector", lambda e, bi=bi: e.tensor_mul(
            sqf[:], xs[bi][:],
            pa[:, None, :].broadcast_to([128, DT, CH])), wait=wo)
        tEv = p.step("vector", lambda e, bi=bi: e.tensor_add(
            xo[bi][:], sqf[:],
            pb[:, None, :].broadcast_to([128, DT, CH])), wait=[], sem="ev")
        tEv_prev = tEv
        freeX[bi] = tEv
        tSq_prev = tEv
        tSt = p.dma("sync", xnr[:, :, sl], xo[bi][:], wait=[tEv],
                    sem="p9")
        freeO[bi] = tSt
    return


def ph_proj(p, C, *, name, act, w, MO, KO, T, TG, evac, act_dt=BF16,
            w_f32_src=True):
    acts = p.sbuf(f"{name}_a", [128, KO, TG], act_dt)
    NWB = 3
    wb = [p.sbuf(f"{name}_w{i}", [128, KO * 128], BF16) for i in range(NWB)]
    ps = [p.psum(f"{name}_p{i}", [128, CH]) for i in range(2)]
    ar = act.rearrange("a p t -> p a t")
    freeW = [None] * NWB
    freePs = [None, None]
    lastE = None
    for g in range(T // TG):
        wA = [lastE] if lastE else []
        tA = p.dma("sync", acts[:], ar[:, :, g * TG:(g + 1) * TG], wait=wA,
                   sem="p15")
        for mo in range(MO):
            wi = mo % NWB
            ww = [t for t in (freeW[wi],) if t]
            tW = p.dma("gpsimd", wb[wi][:], w[mo], wait=ww,
                       sem=f"p{12 + wi}")
            for tc in range(TG // CH):
                pi = (mo * (TG // CH) + tc) % 2
                sl = slice(tc * CH, (tc + 1) * CH)

                def mms(e, wi=wi, sl=sl, pi=pi):
                    last = None
                    for ko in range(KO):
                        last = e.matmul(
                            ps[pi][:], wb[wi][:, ko * 128:(ko + 1) * 128],
                            acts[:, ko, sl],
                            start=(ko == 0), stop=(ko == KO - 1))
                    return last
                wm = [tA, tW] + ([freePs[pi]] if freePs[pi] else [])
                tM = p.step("tensor", mms, wait=wm, sem="mm")
                tE = evac(p, mo, g * TG, sl, ps[pi], tM)
                freePs[pi] = tE
                lastE = tE
                if tc == TG // CH - 1:
                    freeW[wi] = tE
    return


def evac_store_bf16(dst, buf2, name):
    state = {"i": 0, "free": [None, None]}

    def evac(p, mo, t0, sl, psum, tM):
        i = state["i"] % 2
        state["i"] += 1
        w = [tM] + ([state["free"][i]] if state["free"][i] else [])
        tC = p.step("scalar", lambda e: e.copy(buf2[i][:], psum[:]),
                    wait=w, sem="es")
        tS = p.dma("sync", dst[mo][:, t0 + sl.start:t0 + sl.stop],
                   buf2[i][:], wait=[tC], sem=f"p{16 + i}")
        state["free"][i] = tS
        return tC
    return evac


def ph_kvprep(p, C, *, kv, kvn, csum, T, tag="", cs2=None, T1=None):
    kvb = [p.sbuf(f"kvp_i{i}{tag}", [128, T], BF16) for i in range(2)]
    sqb = p.sbuf(f"kvp_sq{tag}", [128, T], BF16)
    nob = [p.sbuf(f"kvp_o{i}{tag}", [128, T], BF16) for i in range(2)]
    ksr = p.sbuf(f"kvp_ks{tag}", [1, T], BF16)
    ksrf = p.sbuf(f"kvp_ksf{tag}", [1, T], F32)
    srow = p.sbuf(f"kvp_sr{tag}", [1, T], F32)
    cs = p.sbuf(f"kvp_cs{tag}", [128, NH], F32)
    pn = p.psum(f"kvp_pn{tag}", [1, CH])
    pb = [p.psum(f"kvp_pb{i}{tag}", [128, CH]) for i in range(2)]
    NCk = T // CH
    freeI = [None, None]
    freeO = [None, None]
    tSq_prev = None
    tPn_free = None
    freeB = [None, None]
    for h in range(NH):
        bi = h % 2
        w = [t for t in (freeI[bi],) if t]
        tL = p.dma("sync", kvb[bi][:], kv[h], wait=w, sem=f"p{8 + bi}")
        wq = [tL] + ([tSq_prev] if tSq_prev else [])
        tQ = p.step("vector", lambda e, bi=bi: e.tensor_mul(
            sqb[:], kvb[bi][:], kvb[bi][:]), wait=wq, sem="ev")
        tRow = None
        for tc in range(NCk):
            sl = slice(tc * CH, (tc + 1) * CH)
            wm = [tQ] + ([tPn_free] if tPn_free else [])
            tM = p.step("tensor", lambda e, sl=sl: e.matmul(
                pn[:], C.ones_col_b[:], sqb[:, sl], start=True, stop=True),
                wait=wm, sem="mm")
            tPn_free = p.step("scalar", lambda e, sl=sl: e.activation(
                srow[:, sl], pn[:], AF.Sqrt, scale=128.0), wait=[tM], sem="es")
            tRow = tPn_free
        tSq_prev = tRow
        vs = VSync(p.S)
        def rowk(e, vs=vs):
            vs.g(e, e.reciprocal(ksrf[:], srow[:]))
            return e.tensor_copy(ksr[:], ksrf[:])
        tKs = p.step("vector", rowk, wait=[tRow], sem="ev")
        tEvLast = None
        for tc in range(NCk):
            sl = slice(tc * CH, (tc + 1) * CH)
            pbi = tc % 2
            wb_ = [tKs] + ([freeB[pbi]] if freeB[pbi] else [])
            tB = p.step("tensor", lambda e, sl=sl, pbi=pbi: e.matmul(
                pb[pbi][:], C.ones_row_b[:], ksr[:, sl], start=True,
                stop=True), wait=wb_, sem="mm")
            wo = [tB] + ([freeO[bi]] if freeO[bi] and tc == 0 else [])
            tEv = p.step("vector", lambda e, bi=bi, sl=sl, pbi=pbi:
                         e.tensor_mul(nob[bi][:, sl], kvb[bi][:, sl],
                                      pb[pbi][:]), wait=wo, sem="ev")
            freeB[pbi] = tEv
            tEvLast = tEv
        freeI[bi] = tEvLast
        if cs2 is None:
            tCs = p.step("vector", lambda e, bi=bi, h=h: e.reduce_sum(
                cs[:, h:h + 1], nob[bi][:], axis=mybir.AxisListType.X),
                wait=[tEvLast], sem="ev")
        else:
            def red2(e, bi=bi, h=h):
                e.reduce_sum(cs2[0][:, h:h + 1], nob[bi][:, 0:T1],
                             axis=mybir.AxisListType.X)
                return e.reduce_sum(cs2[1][:, h:h + 1], nob[bi][:, T1:T],
                                    axis=mybir.AxisListType.X)
            tCs = p.step("vector", red2, wait=[tEvLast], sem="ev")
        tSt = p.dma("sync", kvn[h], nob[bi][:], wait=[tCs],
                    sem=f"p{10 + bi}")
        freeO[bi] = tSt
    if cs2 is None:
        p.dma("sync", csum, cs[:], wait=[tCs])
    return


def build_launch1(T):
    nc = bass.Bass("TRN2", target_bir_lowering=False)
    cd = declare_consts(nc)
    x = nc.dram_tensor("x", [DT, 128, T], F32, kind="ExternalInput")
    wkv = nc.dram_tensor("wkv", [NH, 128, DT * 128], F32, kind="ExternalInput")
    xn = nc.dram_tensor("xn", [DT, 128, T], BF16, kind="ExternalOutput")
    kvn = nc.dram_tensor("kvn", [NH, 128, T], BF16, kind="ExternalOutput")
    csum = nc.dram_tensor("csum", [128, NH], F32, kind="ExternalOutput")
    mrow = nc.dram_tensor("mrow", [1, T], F32, kind="ExternalOutput")
    rrow = nc.dram_tensor("rrow", [1, T], F32, kind="ExternalOutput")
    kv = nc.dram_tensor("kv_scratch", [NH, 128, T], BF16, kind="Internal")

    sems = Sems(nc, SEM_NAMES)
    C = Consts(nc)
    mean_row = nc.alloc_sbuf_tensor("mean_row", [1, T], F32)
    rstd_row = nc.alloc_sbuf_tensor("rstd_row", [1, T], F32)

    run_phase(nc, sems, lambda p: C.load(p, cd))
    run_phase(nc, sems, lambda p: ph_ln(p, C, x=x.ap(), xn=xn.ap(), mean_row=mean_row,
                                        rstd_row=rstd_row, T=T, tag="0"))

    def proj_phase(p):
        buf2 = [p.sbuf(f"kvev{i}", [128, CH], BF16) for i in range(2)]
        ph_proj(p, C, name="kvp", act=xn.ap(), w=wkv.ap(), MO=NH, KO=DT, T=T,
                TG=min(T, 2048), evac=evac_store_bf16(kv.ap(), buf2, "kv"))
    run_phase(nc, sems, proj_phase)
    run_phase(nc, sems, lambda p: ph_kvprep(p, C, kv=kv.ap(), kvn=kvn.ap(),
                                            csum=csum.ap(), T=T))

    def rows_out(p):
        t1 = p.dma("sync", mrow.ap(), mean_row[:])
        t2 = p.dma("sync", rrow.ap(), rstd_row[:])
        p.on("sync", lambda e: None, wait=[t1, t2])
    run_phase(nc, sems, rows_out)
    finish_launch(nc, sems)
    return nc


def ph_fft(p, C, *, q, kvn, vh, carry, T, tag="", carry1=None, T1=None):
    NCk = T // CH
    qb = [p.sbuf(f"ff_q{i}{tag}", [128, T], BF16) for i in range(2)]
    kb = [p.sbuf(f"ff_k{i}{tag}", [128, T], BF16) for i in range(2)]
    Kb = p.sbuf(f"ff_K{tag}", [128, T], BF16)
    vhb = [p.sbuf(f"ff_v{i}{tag}", [128, T], BF16) for i in range(2)]
    sqb = p.sbuf(f"ff_sq{tag}", [128, T], BF16)
    srow = p.sbuf(f"ff_sr{tag}", [1, T], F32)
    qsf = p.sbuf(f"ff_qsf{tag}", [1, T], F32)
    qsb = p.sbuf(f"ff_qs{tag}", [1, T], BF16)
    cqs = p.sbuf(f"ff_cq{tag}", [128, CH], BF16)
    sqs = p.sbuf(f"ff_sqs{tag}", [128, CH], BF16)
    t1f = p.sbuf(f"ff_t1{tag}", [128, CH], F32)
    t2f = p.sbuf(f"ff_t2{tag}", [128, CH], F32)
    reb = p.sbuf(f"ff_re{tag}", [128, CH], BF16)
    imb = p.sbuf(f"ff_im{tag}", [128, CH], BF16)
    tvs = p.sbuf(f"ff_tv{tag}", [128, CH], BF16)
    pn = p.psum(f"ff_pn{tag}", [1, CH])
    pC = p.psum(f"ff_pC{tag}", [128, CH])
    pS = p.psum(f"ff_pS{tag}", [128, CH])
    pCK = p.psum(f"ff_pCK{tag}", [128, CH])
    pSK = p.psum(f"ff_pSK{tag}", [128, CH])
    pv = p.psum(f"ff_pv{tag}", [128, CH])
    pq = p.psum(f"ff_pq{tag}", [128, CH])

    freeQ = [None, None]
    freeK = [None, None]
    freeV = [None, None]
    tPn_free = None
    tSq_free = None
    tScan_free = None
    fC = fS = fCK = fSK = fV = fQb = None
    for h in range(NH):
        bi = h % 2
        tLq = p.dma("sync", qb[bi][:], q[h],
                    wait=[t for t in (freeQ[bi],) if t], sem=f"p{8 + bi}")
        tLk = p.dma("sync", kb[bi][:], kvn[h],
                    wait=[t for t in (freeK[bi],) if t], sem=f"p{10 + bi}")
        wq = [tLq] + ([tSq_free] if tSq_free else [])
        tQ2 = p.step("vector", lambda e, bi=bi: e.tensor_mul(
            sqb[:], qb[bi][:], qb[bi][:]), wait=wq, sem="ev")
        tRow = None
        for tc in range(NCk):
            sl = slice(tc * CH, (tc + 1) * CH)
            wm = [tQ2] + ([tPn_free] if tPn_free else [])
            tM = p.step("tensor", lambda e, sl=sl: e.matmul(
                pn[:], C.ones_col_b[:], sqb[:, sl], start=True, stop=True),
                wait=wm, sem="mm")
            tPn_free = p.step("scalar", lambda e, sl=sl: e.activation(
                srow[:, sl], pn[:], AF.Sqrt, scale=128.0), wait=[tM], sem="es")
            tRow = tPn_free
        tSq_free = tRow
        vs = VSync(p.S)
        def rowq(e, vs=vs):
            vs.g(e, e.reciprocal(qsf[:], srow[:]))
            vs.g(e, e.tensor_scalar(qsf[:], qsf[:], 1.0 / 128.0, None,
                                    ALU.mult))
            return e.tensor_copy(qsb[:], qsf[:])
        tQs = p.step("vector", rowq, wait=[tRow], sem="ev")
        wsc = [tLk] + ([tScan_free] if tScan_free else [])
        if carry1 is None:
            tK = p.step("vector", lambda e, bi=bi, h=h: e.tensor_tensor_scan(
                Kb[:], C.ones_col_b[:].broadcast_to([128, T]), kb[bi][:],
                carry[:, h:h + 1], ALU.mult, ALU.add), wait=wsc, sem="ev")
        else:
            def scan2(e, bi=bi, h=h):
                e.tensor_tensor_scan(
                    Kb[:, 0:T1], C.ones_col_b[:].broadcast_to([128, T1]),
                    kb[bi][:, 0:T1], carry[:, h:h + 1], ALU.mult, ALU.add)
                return e.tensor_tensor_scan(
                    Kb[:, T1:T], C.ones_col_b[:].broadcast_to([128, T - T1]),
                    kb[bi][:, T1:T], carry1[:, h:h + 1], ALU.mult, ALU.add)
            tK = p.step("vector", scan2, wait=wsc, sem="ev")
        freeK[bi] = tK
        tVh_last = None
        for tc in range(NCk):
            sl = slice(tc * CH, (tc + 1) * CH)

            def mmdft(e, bi=bi, sl=sl):
                e.matmul(pC[:], C.dftC[:], qb[bi][:, sl], start=True, stop=True)
                e.matmul(pS[:], C.dftS[:], qb[bi][:, sl], start=True, stop=True)
                e.matmul(pCK[:], C.dftC[:], Kb[:, sl], start=True, stop=True)
                return e.matmul(pSK[:], C.dftS[:], Kb[:, sl], start=True,
                                stop=True)
            wdft = [tK] + [t for t in (fC, fS, fCK, fSK) if t]
            tDF = p.step("tensor", mmdft, wait=wdft, sem="mm")

            def evq(e):
                e.copy(cqs[:], pC[:])
                return e.copy(sqs[:], pS[:])
            tEq = p.step("scalar", evq, wait=[tDF], sem="es")

            def vops(e):
                e.tensor_mul(t1f[:], cqs[:], pCK[:])
                e.tensor_mul(t2f[:], sqs[:], pSK[:])
                e.tensor_sub(reb[:], t1f[:], t2f[:])
                e.tensor_mul(t1f[:], cqs[:], pSK[:])
                e.tensor_mul(t2f[:], sqs[:], pCK[:])
                return e.tensor_add(imb[:], t1f[:], t2f[:])
            tEw = p.step("vector", vops, wait=[tEq], sem="ev")
            fC = fS = fCK = fSK = tEw

            def mmidft(e, sl=sl):
                e.matmul(pv[:], C.dftC[:], reb[:], start=True, stop=False)
                e.matmul(pv[:], C.dftS[:], imb[:], start=False, stop=True)
                return e.matmul(pq[:], C.ones_row_b[:], qsb[:, sl],
                                start=True, stop=True)
            wid = [tEw, tQs] + [t for t in (fV, fQb) if t]
            tID = p.step("tensor", mmidft, wait=wid, sem="mm")
            tTv = p.step("scalar", lambda e: e.copy(tvs[:], pv[:]),
                         wait=[tID], sem="es")
            fV = tTv
            wvh = [tTv] + ([freeV[bi]] if freeV[bi] and tc == 0 else [])
            tVh = p.step("vector", lambda e, bi=bi, sl=sl: e.tensor_mul(
                vhb[bi][:, sl], tvs[:], pq[:]), wait=wvh, sem="ev")
            fQb = tVh
            tVh_last = tVh
        freeQ[bi] = tVh_last
        tScan_free = tVh_last
        tSt = p.dma("sync", vh[h], vhb[bi][:], wait=[tVh_last],
                    sem=f"p{6 + bi}")
        freeV[bi] = tSt
    return


def bcast_rows_sb(p, C, *, rows, dst, T, psname, wait=(), f32=False):
    pb = p.psum(psname, [128, CH])
    ones = C.ones_row_f if f32 else C.ones_row_b
    tFree = None
    tokens = []
    for tc in range(T // CH):
        sl = slice(tc * CH, (tc + 1) * CH)
        w = list(wait) + ([tFree] if tFree else [])
        tB = p.step("tensor", lambda e, sl=sl: e.matmul(
            pb[:], ones[:], rows[:, sl], start=True, stop=True),
            wait=w, sem="mm")
        tFree = p.step("scalar", lambda e, sl=sl: e.copy(dst[:, sl], pb[:]),
                       wait=[tB], sem="es")
        tokens.append(tFree)
    return tokens[-1]


def ph_router0(p, C, *, x, wr, mean_row, rstd_row, c0b, c1b, T, tag=""):
    xs = [p.sbuf(f"r0x{i}{tag}", [128, DT, CH], F32) for i in range(2)]
    wrd = p.sbuf(f"r0w{tag}", [128, DT, 2], F32)
    wdiff = p.sbuf(f"r0d{tag}", [128, DT], F32)
    gneg = p.sbuf(f"r0g{tag}", [1, 1], F32)
    draw = p.sbuf(f"r0raw{tag}", [1, T], F32)
    r1 = p.sbuf(f"r0r1{tag}", [1, T], F32)
    r2 = p.sbuf(f"r0r2{tag}", [1, T], F32)
    pl = p.psum(f"r0pl{tag}", [1, CH])
    pg = p.psum(f"r0pg{tag}", [1, 1])
    tW = p.dma("sync", wrd[:], wr.rearrange("a p e -> p a e"))
    tD = None
    for d in range(DT):
        tD = p.step("vector", lambda e, d=d: e.tensor_sub(
            wdiff[:, d:d + 1], wrd[:, d, 0:1], wrd[:, d, 1:2]),
            wait=[tW] if d == 0 else [], sem="ev" if d == DT - 1 else None)

    def mmg(e):
        last = None
        for d in range(DT):
            last = e.matmul(pg[:], wdiff[:, d:d + 1], C.ones_col_f[:],
                            start=(d == 0), stop=(d == DT - 1))
        return last
    tG = p.step("tensor", mmg, wait=[tD], sem="mm")
    tGn = p.step("scalar", lambda e: e.mul(gneg[:], pg[:], -1.0),
                 wait=[tG], sem="es")
    freeX = [None, None]
    tEv = None
    for c in range(T // CH):
        sl = slice(c * CH, (c + 1) * CH)
        bi = c % 2
        tL = p.dma("sync", xs[bi][:],
                   x.rearrange("a p t -> p a t")[:, :, sl],
                   wait=[t for t in (freeX[bi],) if t], sem=f"p{8 + bi}")

        def mml(e, bi=bi, sl=sl):
            for d in range(DT):
                e.matmul(pl[:], wdiff[:, d:d + 1], xs[bi][:, d, :],
                         start=(d == 0), stop=False)
            return e.matmul(pl[:], gneg[:], mean_row[:, sl], start=False,
                            stop=True)
        wm = [tL, tGn] + ([tEv] if tEv else [])
        tM = p.step("tensor", mml, wait=wm, sem="mm")
        tEv = p.step("scalar", lambda e, sl=sl: e.copy(draw[:, sl], pl[:]),
                     wait=[tM], sem="es")
        freeX[bi] = tEv
    tDt = p.step("vector", lambda e: e.tensor_mul(r1[:], draw[:], rstd_row[:]),
                 wait=[tEv], sem="ev")
    p.step("scalar", lambda e: e.activation(r2[:], r1[:], AF.Abs),
           wait=[tDt])
    tAb = p.step("vector", lambda e: e.tensor_scalar(
        draw[:], r1[:], 0.0, None, ALU.is_ge), sem="ev")
    tP = p.step("scalar", lambda e: e.activation(r2[:], r2[:], AF.Sigmoid),
                wait=[tAb], sem="es")
    vs = VSync(p.S)
    def rowc(e, vs=vs):
        vs.g(e, e.tensor_mul(r1[:], r2[:], draw[:]))
        vs.g(e, e.tensor_sub(r2[:], r2[:], r1[:]))
        vs.g(e, e.tensor_copy(c0b[:], r1[:]))
        return e.tensor_copy(c1b[:], r2[:])
    tC = p.step("vector", rowc, wait=[tP], sem="ev")
    return tC


def ph_moe0_h(p, C, *, xn, wfc, a0, crow, e, T, tag=""):
    ceb = p.sbuf(f"m0c{tag}", [128, T], BF16)
    tCe = bcast_rows_sb(p, C, rows=crow, dst=ceb, T=T, psname=f"m0pb{tag}")
    rsb = p.sbuf(f"m0r{tag}", [128, CH], BF16)
    lsb = p.sbuf(f"m0l{tag}", [128, CH], BF16)
    ssb = p.sbuf(f"m0s{tag}", [128, CH], BF16)
    ab = [p.sbuf(f"m0a{i}{tag}", [128, CH], BF16) for i in range(2)]
    st = {"free": [None, None], "i": 0}

    def evac(p_, mo, t0, sl, psum, tM):
        i = st["i"] % 2
        st["i"] += 1
        tR = p_.step("scalar", lambda e_: e_.activation(
            rsb[:], psum[:], AF.Relu, scale=0.5), wait=[tM], sem="es")
        tL = p_.step("vector", lambda e_: e_.scalar_tensor_tensor(
            lsb[:], psum[:], 0.5, rsb[:], ALU.mult, ALU.add),
            wait=[tR], sem="ev")
        tS = p_.step("scalar", lambda e_: e_.activation(
            ssb[:], lsb[:], AF.Square), wait=[tL], sem="es")
        w = [tS, tCe] + ([st["free"][i]] if st["free"][i] else [])
        tA = p_.step("vector", lambda e_, sl=sl, t0=t0: e_.tensor_mul(
            ab[i][:], ssb[:], ceb[:, t0 + sl.start:t0 + sl.stop]),
            wait=w, sem="ev")
        tSt = p_.dma("sync", a0[mo][:, t0 + sl.start:t0 + sl.stop],
                     ab[i][:], wait=[tA], sem=f"p{16 + i}")
        st["free"][i] = tSt
        return tL
    ph_proj(p, C, name=f"m0h{tag}", act=xn, w=wfc, MO=64, KO=DT, T=T,
            TG=min(T, 2048), evac=evac)
    return

BIGK_NB = 6


def ph_bigk(p, C, *, srcs, MO, T, evac_group, TS=1024, DSET=4, tag=""):
    NB = BIGK_NB
    wb = [p.sbuf(f"bk_w{i}{tag}", [128, DSET * 128], BF16) for i in range(NB)]
    abuf = [p.sbuf(f"bk_a{i}{tag}", [128, TS], BF16) for i in range(NB)]
    ps = [p.psum(f"bk_p{i}{tag}", [128, TS]) for i in range(DSET)]
    freeW = [None] * NB
    freeA = [None] * NB
    freePs = None
    nk = sum(KO for (_, _, KO) in srcs)
    for tq in range(T // TS):
        tsl = slice(tq * TS, (tq + 1) * TS)
        for mg in range(MO // DSET):
            tMM = None
            k = 0
            for (w_ap, act_ap, KO) in srcs:
                for ic in range(KO):
                    wi = k % NB
                    tW = p.dma("gpsimd", wb[wi][:],
                               w_ap[ic][:, mg * DSET * 128:
                                         (mg + 1) * DSET * 128],
                               wait=[t for t in (freeW[wi],) if t],
                               sem=f"p{wi}")
                    tA = p.dma("sync", abuf[wi][:], act_ap[ic][:, tsl],
                               wait=[t for t in (freeA[wi],) if t],
                               sem=f"p{NB + wi}")

                    def mms(e_, wi=wi, first=(k == 0), fin=(k == nk - 1)):
                        last = None
                        for m in range(DSET):
                            for tc in range(TS // CH):
                                last = e_.matmul(
                                    ps[m][:, tc * CH:(tc + 1) * CH],
                                    wb[wi][:, m * 128:(m + 1) * 128],
                                    abuf[wi][:, tc * CH:(tc + 1) * CH],
                                    start=first, stop=fin)
                        return last
                    wm = [tW, tA] + ([freePs] if freePs and k == 0 else [])
                    tMM = p.step("tensor", mms, wait=wm, sem="mm")
                    freeW[wi] = tMM
                    freeA[wi] = tMM
                    k += 1
            freePs = evac_group(p, tq, mg, ps, tMM)
    return


def mk_combine_evac(p, C, *, s, x, out, T, TS=1024, DSET=4, tag=""):
    gsb = p.sbuf(f"cb_g{tag}", [128, CH], BF16)
    sgb = p.sbuf(f"cb_sg{tag}", [128, CH], BF16)
    ssl = p.sbuf(f"cb_s{tag}", [128, DSET, TS], BF16)
    xsl = p.sbuf(f"cb_x{tag}", [128, DSET, TS], F32)
    ob = [p.sbuf(f"cb_o{i}{tag}", [128, TS], F32) for i in range(DSET)]
    sr = s.rearrange("a p t -> p a t")
    xr = x.rearrange("a p t -> p a t")
    outr = out.rearrange("a p t -> p a t")
    st = {"freeSX": None, "freeO": [None] * DSET, "prevMul": None}

    def evac_group(p_, tq, mg, ps, tMM):
        tsl = slice(tq * TS, (tq + 1) * TS)
        wsx = [st["freeSX"]] if st["freeSX"] else []
        tS = p_.dma("sync", ssl[:], sr[:, mg * DSET:(mg + 1) * DSET, tsl],
                    wait=wsx, sem="p18")
        tX = p_.dma("sync", xsl[:], xr[:, mg * DSET:(mg + 1) * DSET, tsl],
                    wait=wsx, sem="p19")
        tEvLast = None
        for m in range(DSET):
            for tc in range(TS // CH):
                csl = slice(tc * CH, (tc + 1) * CH)
                wg = [tMM] if (m == 0 and tc == 0) else []
                if st["prevMul"]:
                    wg = wg + [st["prevMul"]]
                tG = p_.step("scalar", lambda e_, m=m, csl=csl:
                             e_.activation(gsb[:], ps[m][:, csl], AF.Silu),
                             wait=wg, sem="es")
                st["prevMul"] = p_.step(
                    "vector", lambda e_, m=m, csl=csl: e_.tensor_mul(
                        sgb[:], gsb[:], ssl[:, m, csl]),
                    wait=[tG, tS], sem="ev")
                tO = p_.step("vector", lambda e_, m=m, csl=csl:
                             e_.tensor_add(ob[m][:, csl], sgb[:],
                                           xsl[:, m, csl]),
                             wait=[tX] + ([st["freeO"][m]] if st["freeO"][m]
                                          and tc == 0 else []), sem="ev")
                tEvLast = tO
        st["freeSX"] = tEvLast
        for m in range(DSET):
            st["freeO"][m] = p_.dma("sync", outr[:, mg * DSET + m, tsl],
                                    ob[m][:], wait=[tEvLast],
                                    sem=f"p{20 + m}")
        return tEvLast
    return evac_group


def mk_store_evac(p, C, *, dst, T, TS=1024, DSET=4, tag=""):
    ob = [p.sbuf(f"se_o{i}{tag}", [128, TS], BF16) for i in range(DSET)]
    st = {"freeO": [None] * DSET}

    def evac_group(p_, tq, mg, ps, tMM):
        tsl = slice(tq * TS, (tq + 1) * TS)
        tEvLast = None
        for m in range(DSET):
            w = [tMM] if m == 0 else []
            if st["freeO"][m]:
                w = w + [st["freeO"][m]]
            tEvLast = p_.step("scalar", lambda e_, m=m: e_.copy(
                ob[m][:], ps[m][:]), wait=w, sem="es")
            st["freeO"][m] = p_.dma("sync", dst[mg * DSET + m][:, tsl],
                                    ob[m][:], wait=[tEvLast],
                                    sem=f"p{20 + m}")
        return tEvLast
    return evac_group


def ph_moe0_out(p, C, *, a0, wproj, x, s, x1, T, tag=""):
    ev = mk_combine_evac(p, C, s=s, x=x, out=x1, T=T, tag=tag)
    ph_bigk(p, C, srcs=[(wproj[0], a0[0], 64), (wproj[1], a0[1], 64)],
            MO=DT, T=T, evac_group=ev, tag=tag)


def mk_bias_act_evac(p, C, *, dst, bcols, func, tag=""):
    ob = [p.sbuf(f"ba_o{i}{tag}", [128, CH], BF16) for i in range(2)]
    st = {"i": 0, "free": [None, None]}

    def evac(p_, mo, t0, sl, psum, tM):
        i = st["i"] % 2
        st["i"] += 1
        w = [tM] + ([st["free"][i]] if st["free"][i] else [])
        tC = p_.step("scalar", lambda e_, i=i, mo=mo: e_.activation(
            ob[i][:], psum[:], func, bias=bcols[:, mo:mo + 1]),
            wait=w, sem="es")
        tS = p_.dma("sync", dst[mo][:, t0 + sl.start:t0 + sl.stop],
                    ob[i][:], wait=[tC], sem=f"p{16 + i}")
        st["free"][i] = tS
        return tC
    return evac


def mk_ug_evac(p, C, *, ug, gates, bu_cols, ugcs, T, tag="", regions=None,
               TG=2048):
    ub = p.sbuf(f"ug_u{tag}", [128, CH], BF16)
    gb = [p.sbuf(f"ug_g{i}{tag}", [128, T], BF16) for i in range(2)]
    ob = [p.sbuf(f"ug_o{i}{tag}", [128, CH], BF16) for i in range(2)]
    cst = p.sbuf(f"ug_c{tag}", [128, 1], F32)
    st = {"i": 0, "free": [None, None], "tG": None, "gfree": [None, None],
          "prevU": None, "prevCs": None}

    def evac(p_, mo, t0, sl, psum, tM):
        i = st["i"] % 2
        st["i"] += 1
        gi = mo % 2
        if sl.start == 0:
            st["tG"] = p_.dma("sync", gb[gi][:, 0:min(TG, T)],
                              gates[mo][:, t0:t0 + min(TG, T)],
                              wait=[t for t in (st["gfree"][gi],) if t],
                              sem=f"p{18 + gi}")
        wU = [tM] + ([st["prevU"]] if st["prevU"] else [])
        tU = p_.step("scalar", lambda e_, mo=mo: e_.activation(
            ub[:], psum[:], AF.Identity, bias=bu_cols[:, mo:mo + 1]),
            wait=wU, sem="es")
        st["prevU"] = None
        w = [tU, st["tG"]] + ([st["free"][i]] if st["free"][i] else [])
        tO = p_.step("vector", lambda e_, i=i, gi=gi, mo=mo, sl=sl:
                     e_.tensor_mul(ob[i][:], ub[:], gb[gi][:, sl]),
                     wait=w, sem="ev")
        st["prevU"] = tO
        vs = VSync(p_.S)
        pos = t0 + sl.start
        regs = regions if regions is not None else [(ugcs, 0, T)]
        buf = None
        for (rb, rs, re_) in regs:
            if rs <= pos < re_:
                buf, first = rb, (pos == rs)
        assert buf is not None
        def csacc(e_, i=i, mo=mo, first=first, vs=vs, buf=buf):
            vs.g(e_, e_.reduce_sum(cst[:], ob[i][:],
                                   axis=mybir.AxisListType.X))
            if first:
                return e_.tensor_copy(buf[:, mo:mo + 1], cst[:])
            return e_.tensor_add(buf[:, mo:mo + 1], buf[:, mo:mo + 1],
                                 cst[:])
        tCs = p_.step("vector", csacc, wait=[tO], sem="ev")
        tS = p_.dma("sync", ug[mo][:, t0 + sl.start:t0 + sl.stop],
                    ob[i][:], wait=[tCs], sem=f"p{16 + i}")
        st["free"][i] = tS
        if sl.stop >= min(TG, T):
            st["gfree"][gi] = tO
        return tU
    return evac


def ph_l1mix(p, C, *, xn1, wv, ug, gates, go, carry, bv_cols, T, tag=""):
    acts = p.sbuf(f"lm_a{tag}", [128, DT, T], BF16)
    wb = [p.sbuf(f"lm_w{i}{tag}", [128, DT * 128], BF16) for i in range(2)]
    ugb = [p.sbuf(f"lm_u{i}{tag}", [128, T], BF16) for i in range(2)]
    gb = [p.sbuf(f"lm_g{i}{tag}", [128, T], BF16) for i in range(2)]
    Kb = p.sbuf(f"lm_K{tag}", [128, T], BF16)
    vb = p.sbuf(f"lm_v{tag}", [128, CH], BF16)
    vg = p.sbuf(f"lm_vg{tag}", [128, CH], BF16)
    gob = [p.sbuf(f"lm_o{i}{tag}", [128, T], BF16) for i in range(2)]
    ps = [p.psum(f"lm_p{i}{tag}", [128, CH]) for i in range(2)]
    ar = xn1.rearrange("a p t -> p a t")
    tA = p.dma("sync", acts[:], ar[:, :, 0:T], sem="p15")
    freeW = [None, None]
    freeU = [None, None]
    freeG = [None, None]
    freeO = [None, None]
    freePs = [None, None]
    tScan_free = None
    prevV = None
    for mo in range(DT):
        bi = mo % 2
        tW = p.dma("gpsimd", wb[bi][:], wv[mo],
                   wait=[t for t in (freeW[bi],) if t], sem=f"p{12 + bi}")
        tU = p.dma("sync", ugb[bi][:], ug[mo][:, 0:T],
                   wait=[t for t in (freeU[bi],) if t], sem=f"p{8 + bi}")
        tG = p.dma("sync", gb[bi][:], gates[mo][:, 0:T],
                   wait=[t for t in (freeG[bi],) if t], sem=f"p{10 + bi}")
        wsc = [tU] + ([tScan_free] if tScan_free else [])
        tK = p.step("vector", lambda e, bi=bi, mo=mo: e.tensor_tensor_scan(
            Kb[:], C.ones_col_b[:].broadcast_to([128, T]), ugb[bi][:],
            carry[:, mo:mo + 1], ALU.mult, ALU.add), wait=wsc, sem="ev")
        freeU[bi] = tK
        tLast = None
        for tc in range(T // CH):
            sl = slice(tc * CH, (tc + 1) * CH)
            pi = tc % 2

            def mms(e, bi=bi, sl=sl, pi=pi):
                last = None
                for ko in range(DT):
                    last = e.matmul(ps[pi][:], wb[bi][:, ko * 128:(ko + 1) * 128],
                                    acts[:, ko, sl],
                                    start=(ko == 0), stop=(ko == DT - 1))
                return last
            wm = [tA, tW] + ([freePs[pi]] if freePs[pi] else [])
            tM = p.step("tensor", mms, wait=wm, sem="mm")
            wv_ = [tM] + ([prevV] if prevV else [])
            tV = p.step("scalar", lambda e, pi=pi, mo=mo: e.activation(
                vb[:], ps[pi][:], AF.Identity, bias=bv_cols[:, mo:mo + 1]),
                wait=wv_, sem="es")
            freePs[pi] = tV
            tVg = p.step("vector", lambda e, bi=bi, sl=sl: e.tensor_mul(
                vg[:], vb[:], gb[bi][:, sl]), wait=[tV, tG], sem="ev")
            prevV = tVg
            wgo = [tK, tVg] + ([freeO[bi]] if freeO[bi] and tc == 0 else [])
            tGo = p.step("vector", lambda e, bi=bi, sl=sl: e.tensor_mul(
                gob[bi][:, sl], Kb[:, sl], vg[:]), wait=wgo, sem="ev")
            tLast = tGo
        freeG[bi] = tLast
        tScan_free = tLast
        freeW[bi] = tLast
        tSt = p.dma("sync", go[mo], gob[bi][:], wait=[tLast],
                    sem=f"p{6 + bi}")
        freeO[bi] = tSt
    return


def mk_uvconv_evac(p, C, *, xn1, s1, w0c, w1c, w2c, bcc, hl, hr, T, tag="",
                   flags=None, Tfull=None):
    xc = [p.sbuf(f"uc_x{i}{tag}", [128, T + 2], BF16) for i in range(2)]
    c1 = p.sbuf(f"uc_c1{tag}", [128, CH], F32)
    c2 = p.sbuf(f"uc_c2{tag}", [128, CH], F32)
    lcol = p.sbuf(f"uc_l{tag}", [128, 1], BF16)
    zcol = p.sbuf(f"uc_z{tag}", [128, 1], BF16)
    ob = [p.sbuf(f"uc_o{i}{tag}", [128, CH], BF16) for i in range(2)]
    p.step("vector", lambda e: e.memset(zcol[:], 0.0))
    st = {"i": 0, "free": [None, None], "xfree": [None, None], "tX": None,
          "prevC": None}

    def evac(p_, mo, t0, sl, psum, tM):
        i = st["i"] % 2
        st["i"] += 1
        xi = mo % 2
        if t0 + sl.start == 0:
            wx = [t for t in (st["xfree"][xi],) if t]
            if flags is None:
                p_.on("vector", lambda e_, xi=xi, mo=mo: e_.tensor_copy(
                    xc[xi][:, 0:1], hl[:, mo:mo + 1]), wait=wx)
                p_.on("vector", lambda e_, xi=xi, mo=mo: e_.tensor_copy(
                    xc[xi][:, T + 1:T + 2], hr[:, mo:mo + 1]))
                st["tX"] = p_.dma("sync", xc[xi][:, 1:T + 1], xn1[mo],
                                  sem=f"p{18 + xi}")
            else:
                fprev, fnext = flags
                tXi = p_.dma("sync", xc[xi][:, 1:T + 2],
                             xn1[mo][:, 0:T + 1], wait=wx, sem=f"p{18 + xi}")
                tXl = p_.dma("sync", lcol[:],
                             xn1[mo][:, Tfull - 1:Tfull], wait=wx,
                             slow=True, sem="p11")
                def halos(e_, xi=xi):
                    e_.scalar_tensor_tensor(xc[xi][:, 0:1], lcol[:],
                                            fprev[:], zcol[:], ALU.mult,
                                            ALU.add)
                    return e_.scalar_tensor_tensor(
                        xc[xi][:, T + 1:T + 2], xc[xi][:, T + 1:T + 2],
                        fnext[:], zcol[:], ALU.mult, ALU.add)
                st["tX"] = p_.step("vector", halos, wait=[tXi, tXl],
                                   sem="ev")
        a, b_ = t0 + sl.start, t0 + sl.stop
        wc = [tM, st["tX"]] + ([st["prevC"]] if st["prevC"] else [])
        p_.step("vector", lambda e_, xi=xi, mo=mo, a=a, b_=b_:
                e_.scalar_tensor_tensor(c1[:], xc[xi][:, a:b_],
                                        w0c[:, mo:mo + 1], psum[:],
                                        ALU.mult, ALU.add), wait=wc)
        p_.step("vector", lambda e_, xi=xi, mo=mo, a=a, b_=b_:
                e_.scalar_tensor_tensor(c2[:], xc[xi][:, a + 1:b_ + 1],
                                        w1c[:, mo:mo + 1], c1[:],
                                        ALU.mult, ALU.add))
        tC = p_.step("vector", lambda e_, xi=xi, mo=mo, a=a, b_=b_:
                     e_.scalar_tensor_tensor(c1[:], xc[xi][:, a + 2:b_ + 2],
                                             w2c[:, mo:mo + 1], c2[:],
                                             ALU.mult, ALU.add), sem="ev")
        st["prevC"] = None
        w = [tC] + ([st["free"][i]] if st["free"][i] else [])
        tO = p_.step("scalar", lambda e_, i=i, mo=mo: e_.activation(
            ob[i][:], c1[:], AF.Identity, bias=bcc[:, mo:mo + 1]),
            wait=w, sem="es")
        st["prevC"] = tO
        tS = p_.dma("sync", s1[mo][:, a:b_], ob[i][:], wait=[tO],
                    sem=f"p{16 + i}")
        st["free"][i] = tS
        if b_ >= T:
            st["xfree"][xi] = tO
        return tO
    return evac


def ph_r32(p, C, *, x1, gw, gwsum_neg, m1row, r1row, r32s, T, tag=""):
    xs = [p.sbuf(f"r3x{i}{tag}", [128, DT, CH], F32) for i in range(2)]
    gwb = p.sbuf(f"r3w{tag}", [128, DT, 32], F32)
    pg = p.psum(f"r3pg{tag}", [32, 1])
    pgt = p.psum(f"r3pt{tag}", [1, 32])
    pl = p.psum(f"r3pl{tag}", [32, CH])
    pb = p.psum(f"r3pb{tag}", [32, CH])
    gsr = p.sbuf(f"r3gs{tag}", [32, 1], F32)
    tW = p.dma("sync", gwb[:], gw.rearrange("a p e -> p a e"))

    def mmg(e):
        last = None
        for d in range(DT):
            last = e.matmul(pg[:], gwb[:, d, :], C.ones_col_f[:],
                            start=(d == 0), stop=(d == DT - 1))
        return last
    tG = p.step("tensor", mmg, wait=[tW], sem="mm")
    tGc = p.step("scalar", lambda e: e.copy(gsr[:], pg[:]), wait=[tG],
                 sem="es")
    tGt = p.step("tensor", lambda e: e.transpose(pgt[:], gsr[:], C.id32[:]),
                 wait=[tGc], sem="mm")
    tGn = p.step("scalar", lambda e: e.mul(gwsum_neg[:], pgt[:], -1.0),
                 wait=[tGt], sem="es")
    freeX = [None, None]
    tEv = None
    tScl = None
    for c in range(T // CH):
        sl = slice(c * CH, (c + 1) * CH)
        bi = c % 2
        tL = p.dma("sync", xs[bi][:],
                   x1.rearrange("a p t -> p a t")[:, :, sl],
                   wait=[t for t in (freeX[bi],) if t], sem=f"p{8 + bi}")

        def mml(e, bi=bi, sl=sl):
            for d in range(DT):
                e.matmul(pl[:], gwb[:, d, :], xs[bi][:, d, :],
                         start=(d == 0), stop=False)
            return e.matmul(pl[:], gwsum_neg[:], m1row[:, sl], start=False,
                            stop=True)
        wm = [tL, tGn] + ([tEv] if tEv else [])
        tM = p.step("tensor", mml, wait=wm, sem="mm")
        wb_ = [tScl] if tScl else []
        tB = p.step("tensor", lambda e, sl=sl: e.matmul(
            pb[:], C.ones_row_f[:, 0:32], r1row[:, sl], start=True,
            stop=True), wait=wb_, sem="mm")
        tEv = p.step("scalar", lambda e, sl=sl: e.copy(r32s[:, sl], pl[:]),
                     wait=[tM], sem="es")
        freeX[bi] = tEv
        tScl = p.step("vector", lambda e, sl=sl: e.tensor_mul(
            r32s[:, sl], r32s[:, sl], pb[:]), wait=[tEv, tB], sem="ev")
    return tScl


def ph_routing1(p, C, *, r32s, gb_row, cw32, idf, T, tag=""):
    BIG = 1.0e30
    pt = p.psum(f"rt_pt{tag}", [128, 32])
    pb = p.psum(f"rt_pb{tag}", [128, 32])
    pback = p.psum(f"rt_pk{tag}", [32, 128])
    sc = p.sbuf(f"rt_sc{tag}", [128, 32], F32)
    sbb = p.sbuf(f"rt_sb{tag}", [128, 32], F32)
    gbb = p.sbuf(f"rt_gb{tag}", [128, 32], F32)
    ones32 = p.sbuf(f"rt_o32{tag}", [128, 32], F32)
    zer32 = p.sbuf(f"rt_z32{tag}", [128, 32], F32)
    em = p.sbuf(f"rt_em{tag}", [128, 32], F32)
    msk = p.sbuf(f"rt_mk{tag}", [128, 32], F32)
    selm = p.sbuf(f"rt_sel{tag}", [128, 32], F32)
    hit = p.sbuf(f"rt_hit{tag}", [128, 32], F32)
    gs = p.sbuf(f"rt_gs{tag}", [128, 4], F32)
    t8 = p.sbuf(f"rt_t8{tag}", [128, 8], F32)
    t4 = p.sbuf(f"rt_t4{tag}", [128, 4], F32)
    m1 = p.sbuf(f"rt_m1{tag}", [128, 1], F32)
    m2 = p.sbuf(f"rt_m2{tag}", [128, 1], F32)
    cwt = p.sbuf(f"rt_cw{tag}", [32, 128], F32)
    tB0 = p.step("tensor", lambda e: e.matmul(
        pb[:], C.ones_row_f[:], gb_row[:], start=True, stop=True), sem="mm")
    tB1 = p.step("scalar", lambda e: e.copy(gbb[:], pb[:]), wait=[tB0],
                 sem="es")
    p.step("vector", lambda e: e.memset(ones32[:], 1.0))
    p.step("vector", lambda e: e.memset(zer32[:], 0.0))
    tPrev = None
    tTfree = None
    tKfree = None
    tVprev = None
    for i in range(T // 128):
        tsl = slice(i * 128, (i + 1) * 128)
        wt = ([tTfree] if tTfree else [])
        tT = p.step("tensor", lambda e, tsl=tsl: e.transpose(
            pt[:], r32s[:, tsl], C.id32[:]), wait=wt, sem="mm")
        tSc = p.step("scalar", lambda e: e.activation(
            sc[:], pt[:], AF.Sigmoid),
            wait=[tT] + ([tVprev] if tVprev else []), sem="es")
        tTfree = tSc

        vs = VSync(p.S)
        def vchain(e, i=i, vs=vs):
            G = lambda inst: vs.g(e, inst)
            G(e.tensor_add(sbb[:], sc[:], gbb[:]))
            for g in range(4):
                gsl = slice(8 * g, 8 * g + 8)
                G(e.reduce_max(m1[:], sbb[:, gsl], axis=mybir.AxisListType.X))
                G(e.scalar_tensor_tensor(hit[:, 0:8], sbb[:, gsl], m1[:],
                                         ones32[:, 0:8], ALU.is_ge, ALU.mult))
                G(e.scalar_tensor_tensor(t8[:], hit[:, 0:8], -BIG, sbb[:, gsl],
                                         ALU.mult, ALU.add))
                G(e.reduce_max(m2[:], t8[:], axis=mybir.AxisListType.X))
                G(e.tensor_add(gs[:, g:g + 1], m1[:], m2[:]))
            G(e.reduce_max(m1[:], gs[:], axis=mybir.AxisListType.X))
            G(e.scalar_tensor_tensor(hit[:, 0:4], gs[:], m1[:], ones32[:, 0:4],
                                     ALU.is_ge, ALU.mult))
            G(e.scalar_tensor_tensor(t4[:], hit[:, 0:4], -BIG, gs[:],
                                     ALU.mult, ALU.add))
            G(e.reduce_max(m2[:], t4[:], axis=mybir.AxisListType.X))
            G(e.scalar_tensor_tensor(gs[:], gs[:], m2[:], ones32[:, 0:4],
                                     ALU.is_ge, ALU.mult))
            G(e.tensor_copy(em[:].rearrange("p (g r) -> p g r", g=4),
                            gs[:, :, None].broadcast_to([128, 4, 8])))
            G(e.tensor_mul(msk[:], sbb[:], em[:]))
            G(e.tensor_scalar(hit[:], em[:], -1.0, BIG, ALU.add, ALU.mult))
            G(e.tensor_add(msk[:], msk[:], hit[:]))
            G(e.memset(selm[:], 0.0))
            for k in range(8):
                G(e.reduce_max(m1[:], msk[:], axis=mybir.AxisListType.X))
                G(e.scalar_tensor_tensor(hit[:], msk[:], m1[:], ones32[:],
                                         ALU.is_ge, ALU.mult))
                G(e.scalar_tensor_tensor(msk[:], hit[:], -BIG, msk[:],
                                         ALU.mult, ALU.add))
                G(e.tensor_add(selm[:], selm[:], hit[:]))
            G(e.tensor_mul(selm[:], selm[:], sc[:]))
            G(e.reduce_sum(m1[:], selm[:], axis=mybir.AxisListType.X))
            G(e.reciprocal(m2[:], m1[:]))
            G(e.tensor_scalar(m2[:], m2[:], 2.5, None, ALU.mult))
            return e.scalar_tensor_tensor(selm[:], selm[:], m2[:], zer32[:],
                                          ALU.mult, ALU.add)
        wv_ = [tSc] + ([tPrev] if tPrev else [])
        tV = p.step("vector", vchain, wait=wv_, sem="ev")
        tVprev = tV
        wk = [tV] + ([tKfree] if tKfree else [])
        tTb = p.step("tensor", lambda e: e.transpose(
            pback[:], selm[:], idf[:]), wait=wk, sem="mm")
        tCw = p.step("scalar", lambda e: e.copy(cwt[:], pback[:]),
                     wait=[tTb], sem="es")
        tKfree = tCw
        tS = p.dma("sync", cw32[:, tsl], cwt[:], wait=[tCw])
        tPrev = tCw
    return


def ph_moe1_h(p, C, *, xlat, w1e, a1e, cwrow_f32, T, tag=""):
    cwb_row = p.sbuf(f"m1r{tag}", [1, T], BF16)
    ceb = p.sbuf(f"m1c{tag}", [128, T], BF16)
    rowf = p.sbuf(f"m1rf{tag}", [1, T], F32)
    tR = p.dma("sync", rowf[:], cwrow_f32)
    tRb = p.step("vector", lambda e: e.tensor_copy(cwb_row[:], rowf[:]),
                 wait=[tR], sem="ev")
    tCe = bcast_rows_sb(p, C, rows=cwb_row, dst=ceb, T=T,
                        psname=f"m1pb{tag}", wait=[tRb])
    rsb = p.sbuf(f"m1s{tag}", [128, CH], BF16)
    sqb = p.sbuf(f"m1q{tag}", [128, CH], BF16)
    ab = [p.sbuf(f"m1a{i}{tag}", [128, CH], BF16) for i in range(2)]
    st = {"i": 0, "free": [None, None], "prevR": None}

    def evac(p_, mo, t0, sl, psum, tM):
        i = st["i"] % 2
        st["i"] += 1
        wR = [tM] + ([st["prevR"]] if st["prevR"] else [])
        tRl = p_.step("scalar", lambda e_: e_.activation(
            rsb[:], psum[:], AF.Relu), wait=wR, sem="es")
        tSq = p_.step("vector", lambda e_: e_.tensor_mul(
            sqb[:], rsb[:], rsb[:]), wait=[tRl], sem="ev")
        st["prevR"] = tSq
        w = [tSq, tCe] + ([st["free"][i]] if st["free"][i] else [])
        tA = p_.step("vector", lambda e_, i=i, sl=sl, t0=t0: e_.tensor_mul(
            ab[i][:], sqb[:], ceb[:, t0 + sl.start:t0 + sl.stop]),
            wait=w, sem="ev")
        tSt = p_.dma("sync", a1e[mo][:, t0 + sl.start:t0 + sl.stop],
                     ab[i][:], wait=[tA], sem=f"p{16 + i}")
        st["free"][i] = tSt
        return tRl
    ph_proj(p, C, name=f"m1h{tag}", act=xlat, w=w1e, MO=8, KO=8, T=T,
            TG=min(T, 2048), evac=evac)
    return


def mk_relu2_evac(p, C, *, dst, tag=""):
    rsb = p.sbuf(f"r2_r{tag}", [128, CH], BF16)
    ob = [p.sbuf(f"r2_o{i}{tag}", [128, CH], BF16) for i in range(2)]
    st = {"i": 0, "free": [None, None], "prevR": None}

    def evac(p_, mo, t0, sl, psum, tM):
        i = st["i"] % 2
        st["i"] += 1
        wR = [tM] + ([st["prevR"]] if st["prevR"] else [])
        tRl = p_.step("scalar", lambda e_: e_.activation(
            rsb[:], psum[:], AF.Relu), wait=wR, sem="es")
        w = [tRl] + ([st["free"][i]] if st["free"][i] else [])
        tSq = p_.step("vector", lambda e_, i=i: e_.tensor_mul(
            ob[i][:], rsb[:], rsb[:]), wait=w, sem="ev")
        st["prevR"] = tSq
        tSt = p_.dma("sync", dst[mo][:, t0 + sl.start:t0 + sl.stop],
                     ob[i][:], wait=[tSq], sem=f"p{16 + i}")
        st["free"][i] = tSt
        return tRl
    return evac


def build_launch3(T):
    nc = bass.Bass("TRN2", target_bir_lowering=False)
    cd = declare_consts(nc)
    x1 = nc.dram_tensor("x1", [DT, 128, T], F32, kind="ExternalInput")
    wg1 = nc.dram_tensor("wg1", [4, 128, DT * 128], F32, kind="ExternalInput")
    wg2 = nc.dram_tensor("wg2", [DT, 128, 4 * 128], F32, kind="ExternalInput")
    wu = nc.dram_tensor("wu", [DT, 128, DT * 128], F32, kind="ExternalInput")
    bg1c = nc.dram_tensor("bg1c", [128, 4], F32, kind="ExternalInput")
    bg2c = nc.dram_tensor("bg2c", [128, DT], F32, kind="ExternalInput")
    buc = nc.dram_tensor("buc", [128, DT], F32, kind="ExternalInput")
    xn1 = nc.dram_tensor("xn1", [DT, 128, T], BF16, kind="ExternalOutput")
    gates = nc.dram_tensor("gates", [DT, 128, T], BF16, kind="ExternalOutput")
    ug = nc.dram_tensor("ug", [DT, 128, T], BF16, kind="ExternalOutput")
    ugcs = nc.dram_tensor("ugcs", [128, DT], F32, kind="ExternalOutput")
    mrow = nc.dram_tensor("mrow", [1, T], F32, kind="ExternalOutput")
    rrow = nc.dram_tensor("rrow", [1, T], F32, kind="ExternalOutput")
    hsc = nc.dram_tensor("h_scratch", [4, 128, T], BF16, kind="Internal")

    sems = Sems(nc, SEM_NAMES)
    C = Consts(nc)
    mean_row = nc.alloc_sbuf_tensor("mean_row", [1, T], F32)
    rstd_row = nc.alloc_sbuf_tensor("rstd_row", [1, T], F32)
    bg1s = nc.alloc_sbuf_tensor("bg1s", [128, 4], F32)
    bg2s = nc.alloc_sbuf_tensor("bg2s", [128, DT], F32)
    bus = nc.alloc_sbuf_tensor("bus", [128, DT], F32)
    ugcs_sb = nc.alloc_sbuf_tensor("ugcs_sb", [128, DT], F32)

    def pre(p):
        C.load(p, cd)
        ts = [p.dma("sync", t[:], d.ap()) for t, d in
              ((bg1s, bg1c), (bg2s, bg2c), (bus, buc))]
        p.on("sync", lambda e: None, wait=ts)
    run_phase(nc, sems, pre)
    run_phase(nc, sems, lambda p: ph_ln(p, C, x=x1.ap(), xn=xn1.ap(),
                                        mean_row=mean_row, rstd_row=rstd_row,
                                        T=T, tag="1"))

    def ph_h(p):
        ev = mk_bias_act_evac(p, C, dst=hsc.ap(), bcols=bg1s, func=AF.Relu,
                              tag="h")
        ph_proj(p, C, name="g1", act=xn1.ap(), w=wg1.ap(), MO=4, KO=DT, T=T,
                TG=min(T, 2048), evac=ev)
    run_phase(nc, sems, ph_h)

    def ph_g(p):
        ev = mk_bias_act_evac(p, C, dst=gates.ap(), bcols=bg2s,
                              func=AF.Sigmoid, tag="g")
        ph_proj(p, C, name="g2", act=hsc.ap(), w=wg2.ap(), MO=DT, KO=4, T=T,
                TG=min(T, 2048), evac=ev)
    run_phase(nc, sems, ph_g)

    def ph_u(p):
        ev = mk_ug_evac(p, C, ug=ug.ap(), gates=gates.ap(), bu_cols=bus,
                        ugcs=ugcs_sb, T=T, tag="u")
        ph_proj(p, C, name="up", act=xn1.ap(), w=wu.ap(), MO=DT, KO=DT, T=T,
                TG=min(T, 2048), evac=ev)
    run_phase(nc, sems, ph_u)

    def outs(p):
        ts = [p.dma("sync", mrow.ap(), mean_row[:]),
              p.dma("sync", rrow.ap(), rstd_row[:]),
              p.dma("sync", ugcs.ap(), ugcs_sb[:])]
        p.on("sync", lambda e: None, wait=ts)
    run_phase(nc, sems, outs)
    finish_launch(nc, sems)
    return nc


def build_launch4(T, debug=False):
    dbg = "ExternalOutput" if debug else "Internal"
    nc = bass.Bass("TRN2", target_bir_lowering=False)
    cd = declare_consts(nc)
    x1 = nc.dram_tensor("x1", [DT, 128, T], F32, kind="ExternalInput")
    xn1 = nc.dram_tensor("xn1", [DT, 128, T], BF16, kind="ExternalInput")
    gates = nc.dram_tensor("gates", [DT, 128, T], BF16, kind="ExternalInput")
    ug = nc.dram_tensor("ug", [DT, 128, T], BF16, kind="ExternalInput")
    carry_in = nc.dram_tensor("carry", [128, DT], F32, kind="ExternalInput")
    hl_in = nc.dram_tensor("hl", [128, DT], BF16, kind="ExternalInput")
    hr_in = nc.dram_tensor("hr", [128, DT], BF16, kind="ExternalInput")
    mrow_in = nc.dram_tensor("mrow", [1, T], F32, kind="ExternalInput")
    rrow_in = nc.dram_tensor("rrow", [1, T], F32, kind="ExternalInput")
    wv = nc.dram_tensor("wv", [DT, 128, DT * 128], F32, kind="ExternalInput")
    wu = nc.dram_tensor("wu", [DT, 128, DT * 128], F32, kind="ExternalInput")
    bvc = nc.dram_tensor("bvc", [128, DT], F32, kind="ExternalInput")
    w0c_in = nc.dram_tensor("w0c", [128, DT], F32, kind="ExternalInput")
    w1c_in = nc.dram_tensor("w1c", [128, DT], F32, kind="ExternalInput")
    w2c_in = nc.dram_tensor("w2c", [128, DT], F32, kind="ExternalInput")
    bcc_in = nc.dram_tensor("bcc", [128, DT], F32, kind="ExternalInput")
    gw = nc.dram_tensor("gw", [DT, 128, 32], F32, kind="ExternalInput")
    gbr = nc.dram_tensor("gbr", [1, 32], F32, kind="ExternalInput")
    identf = nc.dram_tensor("identf", [128, 128], F32, kind="ExternalInput")
    fc1 = nc.dram_tensor("fc1", [8, 128, DT * 128], F32, kind="ExternalInput")
    fc2 = nc.dram_tensor("fc2", [8, 128, DT * 128], F32, kind="ExternalInput")
    w1 = nc.dram_tensor("w1", [32, 8, 128, 8 * 128], F32, kind="ExternalInput")
    w2 = nc.dram_tensor("w2", [32, 8, 128, 8 * 128], F32, kind="ExternalInput")
    shup = nc.dram_tensor("shup", [32, 128, DT * 128], F32,
                          kind="ExternalInput")
    shdn = nc.dram_tensor("shdn", [32, 128, DT * 128], F32,
                          kind="ExternalInput")
    x2 = nc.dram_tensor("x2", [DT, 128, T], F32, kind="ExternalOutput")
    go = nc.dram_tensor("go_scratch", [DT, 128, T], BF16, kind=dbg)
    s1 = nc.dram_tensor("s1_scratch", [DT, 128, T], BF16, kind=dbg)
    cw32 = nc.dram_tensor("cw_scratch", [32, T], F32, kind=dbg)
    xlat = nc.dram_tensor("xlat_scratch", [8, 128, T], BF16, kind=dbg)
    a1 = nc.dram_tensor("a1_scratch", [32, 8, 128, T], BF16, kind="Internal")
    ylat = nc.dram_tensor("ylat_scratch", [8, 128, T], BF16, kind=dbg)
    ash = nc.dram_tensor("ash_scratch", [32, 128, T], BF16, kind="Internal")

    sems = Sems(nc, SEM_NAMES)
    C = Consts(nc)
    A = nc.alloc_sbuf_tensor
    m1row = A("m1row", [1, T], F32)
    r1row = A("r1row", [1, T], F32)
    carry = A("carry_sb", [128, DT], F32)
    hl = A("hl_sb", [128, DT], BF16)
    hr = A("hr_sb", [128, DT], BF16)
    bvs = A("bvs", [128, DT], F32)
    w0s = A("w0s", [128, DT], F32)
    w1s = A("w1s", [128, DT], F32)
    w2s = A("w2s", [128, DT], F32)
    bccs = A("bccs", [128, DT], F32)
    gbs = A("gbs", [1, 32], F32)
    idf = A("idf", [128, 128], F32)
    r32s = A("r32s", [32, T], F32)
    gwsum_neg = A("gwsum_neg", [1, 32], F32)

    def pre(p):
        C.load(p, cd)
        ts = [p.dma("sync", t[:], d.ap()) for t, d in
              ((m1row, mrow_in), (r1row, rrow_in), (carry, carry_in),
               (hl, hl_in), (hr, hr_in), (bvs, bvc), (w0s, w0c_in),
               (w1s, w1c_in), (w2s, w2c_in), (bccs, bcc_in), (gbs, gbr),
               (idf, identf))]
        p.on("sync", lambda e: None, wait=ts)
    run_phase(nc, sems, pre)

    run_phase(nc, sems, lambda p: ph_r32(
        p, C, x1=x1.ap(), gw=gw.ap(), gwsum_neg=gwsum_neg, m1row=m1row,
        r1row=r1row, r32s=r32s, T=T))
    if debug:
        r32d = nc.dram_tensor("r32_dbg", [32, T], F32, kind="ExternalOutput")
        def dbg_r32(p):
            t = p.dma("sync", r32d.ap(), r32s[:])
            p.on("sync", lambda e: None, wait=[t])
        run_phase(nc, sems, dbg_r32)
    run_phase(nc, sems, lambda p: ph_routing1(
        p, C, r32s=r32s, gb_row=gbs, cw32=cw32.ap(), idf=idf, T=T))
    if debug:
        crdbg = nc.dram_tensor("crows_dbg", [2, T0], BF16, kind="ExternalOutput")
        def dbgc(p):
            t1 = p.dma("sync", crdbg.ap()[0:1, :], c0b[:])
            t2 = p.dma("sync", crdbg.ap()[1:2, :], c1b[:])
            p.on("sync", lambda e: None, wait=[t1, t2])
        run_phase(nc, sems, dbgc)
        ugdbg = nc.dram_tensor("ugcar_dbg", [128, 3 * DT], F32,
                               kind="ExternalOutput")
        def dbgp(p):
            t1 = p.dma("sync", ugdbg.ap()[:, 0:DT], ugcar[:])
            t2 = p.dma("sync", ugdbg.ap()[:, DT:2*DT], ugcs0[:])
            t3 = p.dma("sync", ugdbg.ap()[:, 2*DT:3*DT], ugcs1[:])
            p.on("sync", lambda e: None, wait=[t1, t2, t3])
        run_phase(nc, sems, dbgp)
    run_phase(nc, sems, lambda p: ph_l1mix(
        p, C, xn1=xn1.ap(), wv=wv.ap(), ug=ug.ap(), gates=gates.ap(),
        go=go.ap(), carry=carry, bv_cols=bvs, T=T))

    def ph_uv(p):
        ev = mk_uvconv_evac(p, C, xn1=xn1.ap(), s1=s1.ap(), w0c=w0s, w1c=w1s,
                            w2c=w2s, bcc=bccs, hl=hl, hr=hr, T=T)
        ph_proj(p, C, name="uv", act=go.ap(), w=wu.ap(), MO=DT, KO=DT, T=T,
                TG=min(T, 2048), evac=ev)
    run_phase(nc, sems, ph_uv)

    def ph_fc1(p):
        buf2 = [p.sbuf(f"f1ev{i}", [128, CH], BF16) for i in range(2)]
        ph_proj(p, C, name="f1", act=xn1.ap(), w=fc1.ap(), MO=8, KO=DT, T=T,
                TG=min(T, 2048), evac=evac_store_bf16(xlat.ap(), buf2, "f1"))
    run_phase(nc, sems, ph_fc1)

    for e in range(32):
        run_phase(nc, sems, lambda p, e=e: ph_moe1_h(
            p, C, xlat=xlat.ap(), w1e=w1.ap()[e], a1e=a1.ap()[e],
            cwrow_f32=cw32.ap()[e:e + 1, :], T=T, tag=f"E{e}"))

    def ph_y(p):
        ev = mk_store_evac(p, C, dst=ylat.ap(), T=T, tag="y")
        ph_bigk(p, C, srcs=[(w2.ap()[e], a1.ap()[e], 8) for e in range(32)],
                MO=8, T=T, evac_group=ev, DSET=4, tag="y")
    run_phase(nc, sems, ph_y)

    def ph_sh(p):
        ev = mk_relu2_evac(p, C, dst=ash.ap(), tag="sh")
        ph_proj(p, C, name="sh", act=xn1.ap(), w=shup.ap(), MO=32, KO=DT, T=T,
                TG=min(T, 2048), evac=ev)
    run_phase(nc, sems, ph_sh)

    def ph_out(p):
        ev = mk_combine_evac(p, C, s=s1.ap(), x=x1.ap(), out=x2.ap(), T=T,
                             tag="o1")
        ph_bigk(p, C, srcs=[(shdn.ap(), ash.ap(), 32), (fc2.ap(), ylat.ap(), 8)],
                MO=DT, T=T, evac_group=ev, tag="o1")
    run_phase(nc, sems, ph_out)
    finish_launch(nc, sems)
    return nc



def build_fused(T0=4096, T1=2048, debug=False):
    nc = bass.Bass("TRN2", target_bir_lowering=False)
    cd = declare_consts(nc)
    DI = lambda n, s, d: nc.dram_tensor(n, s, d, kind="ExternalInput")
    x = DI("x", [DT, 128, T0], F32)
    fprev_in = DI("fprev", [128, 1], F32)
    fnext_in = DI("fnext", [128, 1], F32)
    wq = DI("wq", [NH, 128, DT * 128], F32)
    wkv = DI("wkv", [NH, 128, DT * 128], F32)
    wo = DI("wo", [DT, 128, DT * 128], F32)
    wr = DI("wr", [DT, 128, 2], F32)
    wfc = DI("wfc", [2, 64, 128, DT * 128], F32)
    wproj = DI("wproj", [2, 64, 128, D], F32)
    wu = DI("wu", [DT, 128, DT * 128], F32)
    wv = DI("wv", [DT, 128, DT * 128], F32)
    wg1 = DI("wg1", [4, 128, DT * 128], F32)
    wg2 = DI("wg2", [DT, 128, 4 * 128], F32)
    buc = DI("buc", [128, DT], F32)
    bvc = DI("bvc", [128, DT], F32)
    bg1c = DI("bg1c", [128, 4], F32)
    bg2c = DI("bg2c", [128, DT], F32)
    w0c_in = DI("w0c", [128, DT], F32)
    w1c_in = DI("w1c", [128, DT], F32)
    w2c_in = DI("w2c", [128, DT], F32)
    bcc_in = DI("bcc", [128, DT], F32)
    gw = DI("gw", [DT, 128, 32], F32)
    gbr = DI("gbr", [1, 32], F32)
    identf = DI("identf", [128, 128], F32)
    fc1 = DI("fc1", [8, 128, DT * 128], F32)
    fc2 = DI("fc2", [8, 128, DT * 128], F32)
    w1 = DI("w1", [32, 8, 128, 8 * 128], F32)
    w2 = DI("w2", [32, 8, 128, 8 * 128], F32)
    shup = DI("shup", [32, 128, DT * 128], F32)
    shdn = DI("shdn", [32, 128, DT * 128], F32)
    x2 = nc.dram_tensor("x2", [DT, 128, T1], F32, kind="ExternalOutput")
    DS = lambda n, s, d: nc.dram_tensor(n, s, d, kind="Internal")
    dbgk2 = "ExternalOutput" if debug else "Internal"
    xn = DS("xn_s", [DT, 128, T0], BF16)
    kv = DS("kv_s", [NH, 128, T0], BF16)
    kvn = DS("kvn_s", [NH, 128, T0], BF16)
    qh = DS("q_s", [NH, 128, T0], BF16)
    vh = nc.dram_tensor("vh_s", [NH, 128, T0], BF16, kind=dbgk2)
    sdr = nc.dram_tensor("s_s", [DT, 128, T0], BF16, kind=dbgk2)
    a0 = nc.dram_tensor("a0_s", [2, 64, 128, T0], BF16, kind=dbgk2)
    dbgk = "ExternalOutput" if debug else "Internal"
    x1 = nc.dram_tensor("x1_s", [DT, 128, T0], F32, kind=dbgk)
    xn1 = nc.dram_tensor("xn1_s", [DT, 128, T0], BF16, kind=dbgk2)
    gates = nc.dram_tensor("gates_s", [DT, 128, T0], BF16, kind=dbgk2)
    ug = nc.dram_tensor("ug_s", [DT, 128, T0], BF16, kind=dbgk2)
    hsc = DS("h_s", [4, 128, T0], BF16)
    go = nc.dram_tensor("go_s", [DT, 128, T1], BF16, kind=dbgk2)
    s1 = nc.dram_tensor("s1_s", [DT, 128, T1], BF16, kind=dbgk2)
    cw32 = nc.dram_tensor("cw_s", [32, T1], F32, kind=dbgk2)
    xlat = DS("xlat_s", [8, 128, T1], BF16)
    a1 = DS("a1_s", [32, 8, 128, T1], BF16)
    ylat = DS("ylat_s", [8, 128, T1], BF16)
    ash = DS("ash_s", [32, 128, T1], BF16)

    sems = Sems(nc, SEM_NAMES)
    C = Consts(nc)
    A = nc.alloc_sbuf_tensor
    m0row = A("m0row", [1, T0], F32)
    r0row = A("r0row", [1, T0], F32)
    m1row = A("m1row", [1, T0], F32)
    r1row = A("r1row", [1, T0], F32)
    c0b = A("c0b", [1, T0], BF16)
    c1b = A("c1b", [1, T0], BF16)
    fprev = A("fprev_sb", [128, 1], F32)
    fnext = A("fnext_sb", [128, 1], F32)
    cs0 = A("cs0", [128, NH], F32)
    cs1 = A("cs1", [128, NH], F32)
    kcar0 = A("kcar0", [128, NH], F32)
    kcar1 = A("kcar1", [128, NH], F32)
    zcs = A("zcs", [128, NH], F32)
    ugcs0 = A("ugcs0", [128, DT], F32)
    ugcs1 = A("ugcs1", [128, DT], F32)
    ugcar = A("ugcar", [128, DT], F32)
    bg1s = A("bg1s", [128, 4], F32)
    bg2s = A("bg2s", [128, DT], F32)
    bus = A("bus", [128, DT], F32)
    bvs = A("bvs", [128, DT], F32)
    w0s = A("w0s", [128, DT], F32)
    w1s = A("w1s", [128, DT], F32)
    w2s = A("w2s", [128, DT], F32)
    bccs = A("bccs", [128, DT], F32)
    gbs = A("gbs", [1, 32], F32)
    idf = A("idf", [128, 128], F32)
    r32s = A("r32s", [32, T1], F32)
    gwsum_neg = A("gwsum_neg", [1, 32], F32)

    def pre(p):
        C.load(p, cd)
        ts = [p.dma("sync", t[:], d.ap()) for t, d in
              ((fprev, fprev_in), (fnext, fnext_in), (bg1s, bg1c),
               (bg2s, bg2c), (bus, buc), (bvs, bvc), (w0s, w0c_in),
               (w1s, w1c_in), (w2s, w2c_in), (bccs, bcc_in), (gbs, gbr),
               (idf, identf))]
        p.step("vector", lambda e: e.memset(zcs[:], 0.0))
        p.on("sync", lambda e: None, wait=ts)
    run_phase(nc, sems, pre)

    run_phase(nc, sems, lambda p: ph_ln(p, C, x=x.ap(), xn=xn.ap(),
                                        mean_row=m0row, rstd_row=r0row,
                                        T=T0, tag="0"))

    def phkv(p):
        buf2 = [p.sbuf(f"kvev{i}", [128, CH], BF16) for i in range(2)]
        ph_proj(p, C, name="kvp", act=xn.ap(), w=wkv.ap(), MO=NH, KO=DT,
                T=T0, TG=2048, evac=evac_store_bf16(kv.ap(), buf2, "kv"))
    run_phase(nc, sems, phkv)
    run_phase(nc, sems, lambda p: ph_kvprep(
        p, C, kv=kv.ap(), kvn=kvn.ap(), csum=None, T=T0, cs2=(cs0, cs1),
        T1=T1))

    def phcar0(p):
        def f(e):
            e.scalar_tensor_tensor(kcar0[:], cs1[:], fprev[:], zcs[:],
                                   ALU.mult, ALU.add)
            return e.scalar_tensor_tensor(kcar1[:], cs0[:], fnext[:],
                                          zcs[:], ALU.mult, ALU.add)
        p.step("vector", f, sem="ev")
    run_phase(nc, sems, phcar0)

    def phq(p):
        buf2 = [p.sbuf(f"qev{i}", [128, CH], BF16) for i in range(2)]
        ph_proj(p, C, name="qp", act=xn.ap(), w=wq.ap(), MO=NH, KO=DT,
                T=T0, TG=2048, evac=evac_store_bf16(qh.ap(), buf2, "q"))
    run_phase(nc, sems, phq)
    run_phase(nc, sems, lambda p: ph_fft(p, C, q=qh.ap(), kvn=kvn.ap(),
                                         vh=vh.ap(), carry=kcar0, T=T0,
                                         carry1=kcar1, T1=T1))

    def pho(p):
        buf2 = [p.sbuf(f"oev{i}", [128, CH], BF16) for i in range(2)]
        ph_proj(p, C, name="op", act=vh.ap(), w=wo.ap(), MO=DT, KO=NH,
                T=T0, TG=2048, evac=evac_store_bf16(sdr.ap(), buf2, "o"))
    run_phase(nc, sems, pho)
    run_phase(nc, sems, lambda p: ph_router0(
        p, C, x=x.ap(), wr=wr.ap(), mean_row=m0row, rstd_row=r0row,
        c0b=c0b, c1b=c1b, T=T0))
    for e in range(2):
        run_phase(nc, sems, lambda p, e=e: ph_moe0_h(
            p, C, xn=xn.ap(), wfc=wfc.ap()[e], a0=a0.ap()[e],
            crow=(c0b if e == 0 else c1b), e=e, T=T0, tag=f"e{e}"))
    run_phase(nc, sems, lambda p: ph_moe0_out(
        p, C, a0=a0.ap(), wproj=wproj.ap(), x=x.ap(), s=sdr.ap(),
        x1=x1.ap(), T=T0, tag="m0o"))

    run_phase(nc, sems, lambda p: ph_ln(p, C, x=x1.ap(), xn=xn1.ap(),
                                        mean_row=m1row, rstd_row=r1row,
                                        T=T0, tag="1"))
    run_phase(nc, sems, lambda p: ph_r32(
        p, C, x1=x1.ap(), gw=gw.ap(), gwsum_neg=gwsum_neg, m1row=m1row,
        r1row=r1row, r32s=r32s, T=T1))
    run_phase(nc, sems, lambda p: ph_routing1(
        p, C, r32s=r32s, gb_row=gbs, cw32=cw32.ap(), idf=idf, T=T1))

    def ph_h(p):
        ev = mk_bias_act_evac(p, C, dst=hsc.ap(), bcols=bg1s, func=AF.Relu,
                              tag="h")
        ph_proj(p, C, name="g1", act=xn1.ap(), w=wg1.ap(), MO=4, KO=DT,
                T=T0, TG=2048, evac=ev)
    run_phase(nc, sems, ph_h)

    def ph_g(p):
        ev = mk_bias_act_evac(p, C, dst=gates.ap(), bcols=bg2s,
                              func=AF.Sigmoid, tag="g")
        ph_proj(p, C, name="g2", act=hsc.ap(), w=wg2.ap(), MO=DT, KO=4,
                T=T0, TG=2048, evac=ev)
    run_phase(nc, sems, ph_g)

    def ph_u(p):
        ev = mk_ug_evac(p, C, ug=ug.ap(), gates=gates.ap(), bu_cols=bus,
                        ugcs=None, T=T0, tag="u",
                        regions=[(ugcs0, 0, T1), (ugcs1, T1, T0)])
        ph_proj(p, C, name="up", act=xn1.ap(), w=wu.ap(), MO=DT, KO=DT,
                T=T0, TG=2048, evac=ev)
    run_phase(nc, sems, ph_u)

    def phcar1(p):
        p.step("vector", lambda e: e.scalar_tensor_tensor(
            ugcar[:], ugcs1[:], fprev[:], zcs[:] if DT == NH else None,
            ALU.mult, ALU.add), sem="ev")
    run_phase(nc, sems, phcar1)

    if debug:
        crdbg = nc.dram_tensor("crows_dbg", [2, T0], BF16, kind="ExternalOutput")
        def dbgc(p):
            t1 = p.dma("sync", crdbg.ap()[0:1, :], c0b[:])
            t2 = p.dma("sync", crdbg.ap()[1:2, :], c1b[:])
            p.on("sync", lambda e: None, wait=[t1, t2])
        run_phase(nc, sems, dbgc)
        ugdbg = nc.dram_tensor("ugcar_dbg", [128, 3 * DT], F32,
                               kind="ExternalOutput")
        def dbgp(p):
            t1 = p.dma("sync", ugdbg.ap()[:, 0:DT], ugcar[:])
            t2 = p.dma("sync", ugdbg.ap()[:, DT:2*DT], ugcs0[:])
            t3 = p.dma("sync", ugdbg.ap()[:, 2*DT:3*DT], ugcs1[:])
            p.on("sync", lambda e: None, wait=[t1, t2, t3])
        run_phase(nc, sems, dbgp)
    run_phase(nc, sems, lambda p: ph_l1mix(
        p, C, xn1=xn1.ap(), wv=wv.ap(), ug=ug.ap(), gates=gates.ap(),
        go=go.ap(), carry=ugcar, bv_cols=bvs, T=T1))

    def ph_uv(p):
        ev = mk_uvconv_evac(p, C, xn1=xn1.ap(), s1=s1.ap(), w0c=w0s,
                            w1c=w1s, w2c=w2s, bcc=bccs, hl=None, hr=None,
                            T=T1, flags=(fprev, fnext), Tfull=T0)
        ph_proj(p, C, name="uv", act=go.ap(), w=wu.ap(), MO=DT, KO=DT,
                T=T1, TG=2048, evac=ev)
    run_phase(nc, sems, ph_uv)

    def ph_fc1(p):
        buf2 = [p.sbuf(f"f1ev{i}", [128, CH], BF16) for i in range(2)]
        ph_proj(p, C, name="f1", act=xn1.ap(), w=fc1.ap(), MO=8, KO=DT,
                T=T1, TG=2048, evac=evac_store_bf16(xlat.ap(), buf2, "f1"))
    run_phase(nc, sems, ph_fc1)

    for e in range(32):
        run_phase(nc, sems, lambda p, e=e: ph_moe1_h(
            p, C, xlat=xlat.ap(), w1e=w1.ap()[e], a1e=a1.ap()[e],
            cwrow_f32=cw32.ap()[e:e + 1, :], T=T1, tag=f"E{e}"))

    def ph_y(p):
        ev = mk_store_evac(p, C, dst=ylat.ap(), T=T1, tag="y")
        ph_bigk(p, C, srcs=[(w2.ap()[e], a1.ap()[e], 8) for e in range(32)],
                MO=8, T=T1, evac_group=ev, DSET=4, tag="y")
    run_phase(nc, sems, ph_y)

    def ph_sh(p):
        ev = mk_relu2_evac(p, C, dst=ash.ap(), tag="sh")
        ph_proj(p, C, name="sh", act=xn1.ap(), w=shup.ap(), MO=32, KO=DT,
                T=T1, TG=2048, evac=ev)
    run_phase(nc, sems, ph_sh)

    def ph_out(p):
        ev = mk_combine_evac(p, C, s=s1.ap(), x=x1.ap(), out=x2.ap(), T=T1,
                             tag="o1")
        ph_bigk(p, C, srcs=[(shdn.ap(), ash.ap(), 32),
                            (fc2.ap(), ylat.ap(), 8)],
                MO=DT, T=T1, evac_group=ev, tag="o1")
    run_phase(nc, sems, ph_out)
    finish_launch(nc, sems)
    return nc


_NCACHE = {}

def _build(name, fn, *a, **k):
    key = (name,) + a + tuple(sorted(k.items()))
    if key not in _NCACHE:
        _NCACHE[key] = fn(*a, **k)
    return _NCACHE[key]


def prep_weights(I):
    cc = lambda a: np.ascontiguousarray(a)
    P = prep_lhsT
    W = {}
    W["wq"] = P(cc(np.asarray(I["L0_attn_W_q"]).T)).reshape(NH, 128, DT * 128)
    W["wkv"] = P(cc(np.asarray(I["L0_attn_W_kv"]).T)).reshape(NH, 128, DT * 128)
    W["wo"] = P(cc(np.asarray(I["L0_attn_W_o"]).T)).reshape(DT, 128, DT * 128)
    W["wr"] = cc(np.asarray(I["L0_ffn_w_router"]).T).reshape(DT, 128, 2)
    W["wfc"] = np.stack([P(cc(np.asarray(I["L0_ffn_w_fc"])[e].T)).reshape(64, 128, DT * 128)
                         for e in range(2)])
    W["wproj"] = np.stack([cc(np.asarray(I["L0_ffn_w_proj"])[e].T).reshape(64, 128, D)
                           for e in range(2)])
    W["wu"] = P(cc(np.asarray(I["L1_attn_Wu"]).T)).reshape(DT, 128, DT * 128)
    W["wv"] = P(cc(np.asarray(I["L1_attn_Wv"]).T)).reshape(DT, 128, DT * 128)
    W["wg1"] = P(cc(np.asarray(I["L1_attn_Wg1"]).T)).reshape(4, 128, DT * 128)
    W["wg2"] = P(cc(np.asarray(I["L1_attn_Wg2"]).T)).reshape(DT, 128, 4 * 128)
    W["buc"] = col_tiles(np.asarray(I["L1_attn_bu"]))
    W["bvc"] = col_tiles(np.asarray(I["L1_attn_bv"]))
    W["bg1c"] = col_tiles(np.asarray(I["L1_attn_bg1"]))
    W["bg2c"] = col_tiles(np.asarray(I["L1_attn_bg2"]))
    cwq = np.asarray(I["L1_attn_conv_w"])
    W["w0c"] = col_tiles(cc(cwq[:, 0, 0]))
    W["w1c"] = col_tiles(cc(cwq[:, 0, 1]))
    W["w2c"] = col_tiles(cc(cwq[:, 0, 2]))
    W["bcc"] = col_tiles(np.asarray(I["L1_attn_conv_b"]))
    W["gw"] = cc(np.asarray(I["L1_ffn_gate_w"]).T).reshape(DT, 128, 32)
    W["gbr"] = np.asarray(I["L1_ffn_gate_bias"])[None, :]
    W["fc1"] = P(cc(np.asarray(I["L1_ffn_fc1_w"]).T)).reshape(8, 128, DT * 128)
    W["fc2"] = cc(np.asarray(I["L1_ffn_fc2_w"]).T).reshape(8, 128, DT * 128)
    W["w1"] = np.stack([P(cc(np.asarray(I["L1_ffn_w1"])[e].T)).reshape(8, 128, 1024)
                        for e in range(32)])
    W["w2"] = np.stack([cc(np.asarray(I["L1_ffn_w2"])[e].T).reshape(8, 128, 1024)
                        for e in range(32)])
    W["shup"] = P(cc(np.asarray(I["L1_ffn_shared_up_w"]).T)).reshape(32, 128, DT * 128)
    W["shdn"] = cc(np.asarray(I["L1_ffn_shared_down_w"]).T).reshape(32, 128, DT * 128)
    W["identf"] = np.eye(128, dtype=np.float32)
    return W


FUSED = True


def kernel_fused(**I):
    T0, T1 = S, S // 2
    W = prep_weights(I)
    cons = const_inputs()
    cc = lambda a: np.ascontiguousarray(a)
    x_full = np.asarray(I["L0_attn_x"])
    nc = _build("fused", build_fused, T0, T1)
    in_maps = []
    onec = np.ones((128, 1), np.float32)
    zeroc = np.zeros((128, 1), np.float32)
    for c in range(8):
        b, hh = c // 2, c % 2
        own = x_full[b, hh * T1:(hh + 1) * T1, :]
        part = x_full[b, (1 - hh) * T1:(2 - hh) * T1, :]
        xarr = f_layout(cc(np.concatenate([own, part], axis=0)))
        in_maps.append({
            "x": xarr,
            "fprev": onec if hh == 1 else zeroc,
            "fnext": onec if hh == 0 else zeroc,
            **W, **cons})
    res = run_bass_kernel_spmd(nc, in_maps, core_ids=list(range(8))).results
    out = np.empty((B, S, D), np.float32)
    for c in range(8):
        b, hh = c // 2, c % 2
        out[b, hh * T1:(hh + 1) * T1, :] = un_f_layout(np.asarray(res[c]["x2"]))
    return out


def kernel_unfused(L0_attn_x, L0_attn_W_q, L0_attn_W_kv, L0_attn_W_o,
           L0_ffn_w_router, L0_ffn_w_fc, L0_ffn_w_proj,
           L1_attn_Wu, L1_attn_bu, L1_attn_Wv, L1_attn_bv,
           L1_attn_Wg1, L1_attn_bg1, L1_attn_Wg2, L1_attn_bg2,
           L1_attn_conv_w, L1_attn_conv_b,
           L1_ffn_gate_w, L1_ffn_gate_bias, L1_ffn_fc1_w, L1_ffn_fc2_w,
           L1_ffn_w1, L1_ffn_w2, L1_ffn_shared_up_w, L1_ffn_shared_down_w):
    T = 2048
    cons = const_inputs()
    cc = lambda a: np.ascontiguousarray(a)
    P = prep_lhsT

    wq = P(cc(np.asarray(L0_attn_W_q).T)).reshape(NH, 128, DT * 128)
    wkv = P(cc(np.asarray(L0_attn_W_kv).T)).reshape(NH, 128, DT * 128)
    wo = P(cc(np.asarray(L0_attn_W_o).T)).reshape(DT, 128, DT * 128)
    wr = cc(np.asarray(L0_ffn_w_router).T).reshape(DT, 128, 2)
    wfc = np.stack([P(cc(np.asarray(L0_ffn_w_fc)[e].T)).reshape(64, 128, DT * 128)
                    for e in range(2)])
    wproj = np.stack([cc(np.asarray(L0_ffn_w_proj)[e].T).reshape(64, 128, D)
                      for e in range(2)])
    wu = P(cc(np.asarray(L1_attn_Wu).T)).reshape(DT, 128, DT * 128)
    wv = P(cc(np.asarray(L1_attn_Wv).T)).reshape(DT, 128, DT * 128)
    wg1 = P(cc(np.asarray(L1_attn_Wg1).T)).reshape(4, 128, DT * 128)
    wg2 = P(cc(np.asarray(L1_attn_Wg2).T)).reshape(DT, 128, 4 * 128)
    buc = col_tiles(np.asarray(L1_attn_bu))
    bvc = col_tiles(np.asarray(L1_attn_bv))
    bg1c = col_tiles(np.asarray(L1_attn_bg1))
    bg2c = col_tiles(np.asarray(L1_attn_bg2))
    cwq = np.asarray(L1_attn_conv_w)
    w0c = col_tiles(cc(cwq[:, 0, 0]))
    w1c = col_tiles(cc(cwq[:, 0, 1]))
    w2c = col_tiles(cc(cwq[:, 0, 2]))
    bcc = col_tiles(np.asarray(L1_attn_conv_b))
    gw = cc(np.asarray(L1_ffn_gate_w).T).reshape(DT, 128, 32)
    gbr = np.asarray(L1_ffn_gate_bias)[None, :]
    fc1 = P(cc(np.asarray(L1_ffn_fc1_w).T)).reshape(8, 128, DT * 128)
    fc2 = cc(np.asarray(L1_ffn_fc2_w).T).reshape(8, 128, DT * 128)
    w1 = np.stack([P(cc(np.asarray(L1_ffn_w1)[e].T)).reshape(8, 128, 1024)
                   for e in range(32)])
    w2 = np.stack([cc(np.asarray(L1_ffn_w2)[e].T).reshape(8, 128, 1024)
                   for e in range(32)])
    shup = P(cc(np.asarray(L1_ffn_shared_up_w).T)).reshape(32, 128, DT * 128)
    shdn = cc(np.asarray(L1_ffn_shared_down_w).T).reshape(32, 128, DT * 128)
    identf = np.eye(128, dtype=np.float32)

    x_full = np.asarray(L0_attn_x)
    xs = []
    for c in range(8):
        b, hh = c // 2, c % 2
        xs.append(f_layout(cc(x_full[b, hh * T:(hh + 1) * T, :])))

    zcol = np.zeros((128, NH), np.float32)

    nc1 = _build("l1", build_launch1, T)
    r1 = run_bass_kernel_spmd(
        nc1, [{"x": xs[c], "wkv": wkv, **cons} for c in range(8)],
        core_ids=list(range(8))).results

    nc2 = _build("l2", build_launch2, T)
    in2 = []
    for c in range(8):
        part = c ^ 1
        carry = np.asarray(r1[part]["csum"]) if c % 2 == 1 else zcol
        in2.append({
            "x": xs[c], "xn": np.asarray(r1[c]["xn"]),
            "kvn": np.asarray(r1[c]["kvn"]), "carry": carry,
            "mrow": np.asarray(r1[c]["mrow"]), "rrow": np.asarray(r1[c]["rrow"]),
            "wq": wq, "wo": wo, "wr": wr, "wfc": wfc, "wproj": wproj, **cons})
    r2 = run_bass_kernel_spmd(nc2, in2, core_ids=list(range(8))).results

    nc3 = _build("l3", build_launch3, T)
    in3 = [{"x1": np.asarray(r2[c]["x1"]), "wg1": wg1, "wg2": wg2, "wu": wu,
            "bg1c": bg1c, "bg2c": bg2c, "buc": buc, **cons} for c in range(8)]
    r3 = run_bass_kernel_spmd(nc3, in3, core_ids=list(range(8))).results

    nc4 = _build("l4", build_launch4, T)
    in4 = []
    zb = np.zeros((128, DT), bf16)
    for c in range(8):
        part = c ^ 1
        if c % 2 == 1:
            carry = np.asarray(r3[part]["ugcs"])
            hl = cc(np.asarray(r3[part]["xn1"])[:, :, T - 1].T)
            hr = zb
        else:
            carry = np.zeros((128, DT), np.float32)
            hl = zb
            hr = cc(np.asarray(r3[part]["xn1"])[:, :, 0].T)
        in4.append({
            "x1": np.asarray(r2[c]["x1"]), "xn1": np.asarray(r3[c]["xn1"]),
            "gates": np.asarray(r3[c]["gates"]), "ug": np.asarray(r3[c]["ug"]),
            "carry": carry, "hl": hl, "hr": hr,
            "mrow": np.asarray(r3[c]["mrow"]), "rrow": np.asarray(r3[c]["rrow"]),
            "wv": wv, "wu": wu, "bvc": bvc, "w0c": w0c, "w1c": w1c,
            "w2c": w2c, "bcc": bcc, "gw": gw, "gbr": gbr, "identf": identf,
            "fc1": fc1, "fc2": fc2, "w1": w1, "w2": w2, "shup": shup,
            "shdn": shdn, **cons})
    r4 = run_bass_kernel_spmd(nc4, in4, core_ids=list(range(8))).results

    out = np.empty((B, S, D), np.float32)
    for c in range(8):
        b, hh = c // 2, c % 2
        out[b, hh * T:(hh + 1) * T, :] = un_f_layout(np.asarray(r4[c]["x2"]))
    return out


def build_launch2(T, debug=False):
    dbg = "ExternalOutput" if debug else "Internal"
    nc = bass.Bass("TRN2", target_bir_lowering=False)
    cd = declare_consts(nc)
    x = nc.dram_tensor("x", [DT, 128, T], F32, kind="ExternalInput")
    xn = nc.dram_tensor("xn", [DT, 128, T], BF16, kind="ExternalInput")
    kvn = nc.dram_tensor("kvn", [NH, 128, T], BF16, kind="ExternalInput")
    carry_in = nc.dram_tensor("carry", [128, NH], F32, kind="ExternalInput")
    mrow_in = nc.dram_tensor("mrow", [1, T], F32, kind="ExternalInput")
    rrow_in = nc.dram_tensor("rrow", [1, T], F32, kind="ExternalInput")
    wq = nc.dram_tensor("wq", [NH, 128, DT * 128], F32, kind="ExternalInput")
    wo = nc.dram_tensor("wo", [DT, 128, DT * 128], F32, kind="ExternalInput")
    wr = nc.dram_tensor("wr", [DT, 128, 2], F32, kind="ExternalInput")
    wfc = nc.dram_tensor("wfc", [2, 64, 128, DT * 128], F32,
                         kind="ExternalInput")
    wproj = nc.dram_tensor("wproj", [2, 64, 128, D], F32, kind="ExternalInput")
    x1 = nc.dram_tensor("x1", [DT, 128, T], F32, kind="ExternalOutput")
    qh = nc.dram_tensor("q_scratch", [NH, 128, T], BF16, kind=dbg)
    vh = nc.dram_tensor("vh_scratch", [NH, 128, T], BF16, kind=dbg)
    sdr = nc.dram_tensor("s_scratch", [DT, 128, T], BF16, kind=dbg)
    a0 = nc.dram_tensor("a0_scratch", [2, 64, 128, T], BF16, kind=dbg)
    c0o = nc.dram_tensor("c0_dbg", [1, T], BF16, kind=dbg) if debug else None
    c1o = nc.dram_tensor("c1_dbg", [1, T], BF16, kind=dbg) if debug else None

    sems = Sems(nc, SEM_NAMES)
    C = Consts(nc)
    mean_row = nc.alloc_sbuf_tensor("mean_row", [1, T], F32)
    rstd_row = nc.alloc_sbuf_tensor("rstd_row", [1, T], F32)
    carry = nc.alloc_sbuf_tensor("carry_sb", [128, NH], F32)
    c0b = nc.alloc_sbuf_tensor("c0b", [1, T], BF16)
    c1b = nc.alloc_sbuf_tensor("c1b", [1, T], BF16)

    def ph_pre(p):
        C.load(p, cd)
        t1 = p.dma("sync", mean_row[:], mrow_in.ap())
        t2 = p.dma("sync", rstd_row[:], rrow_in.ap())
        t3 = p.dma("sync", carry[:], carry_in.ap())
        p.on("sync", lambda e: None, wait=[t1, t2, t3])
    run_phase(nc, sems, ph_pre)

    def ph_q(p):
        buf2 = [p.sbuf(f"qev{i}", [128, CH], BF16) for i in range(2)]
        ph_proj(p, C, name="qp", act=xn.ap(), w=wq.ap(), MO=NH, KO=DT, T=T,
                TG=min(T, 2048), evac=evac_store_bf16(qh.ap(), buf2, "q"))
    run_phase(nc, sems, ph_q)
    run_phase(nc, sems, lambda p: ph_fft(p, C, q=qh.ap(), kvn=kvn.ap(),
                                         vh=vh.ap(), carry=carry, T=T))

    def ph_o(p):
        buf2 = [p.sbuf(f"oev{i}", [128, CH], BF16) for i in range(2)]
        ph_proj(p, C, name="op", act=vh.ap(), w=wo.ap(), MO=DT, KO=NH, T=T,
                TG=min(T, 2048), evac=evac_store_bf16(sdr.ap(), buf2, "o"))
    run_phase(nc, sems, ph_o)
    run_phase(nc, sems, lambda p: ph_router0(
        p, C, x=x.ap(), wr=wr.ap(), mean_row=mean_row, rstd_row=rstd_row,
        c0b=c0b, c1b=c1b, T=T))
    if debug:
        def dbg_rows(p):
            t1 = p.dma("sync", c0o.ap(), c0b[:])
            t2 = p.dma("sync", c1o.ap(), c1b[:])
            p.on("sync", lambda e: None, wait=[t1, t2])
        run_phase(nc, sems, dbg_rows)
    for e in range(2):
        run_phase(nc, sems, lambda p, e=e: ph_moe0_h(
            p, C, xn=xn.ap(), wfc=wfc.ap()[e], a0=a0.ap()[e],
            crow=(c0b if e == 0 else c1b), e=e, T=T, tag=f"e{e}"))
    run_phase(nc, sems, lambda p: ph_moe0_out(
        p, C, a0=a0.ap(), wproj=wproj.ap(), x=x.ap(), s=sdr.ap(), x1=x1.ap(),
        T=T))
    finish_launch(nc, sems)
    return nc


def kernel(**inputs):
    if FUSED:
        return kernel_fused(**inputs)
    return kernel_unfused(**inputs)
